# version 21 (final)
# baseline (speedup 1.0000x reference)
import numpy as np
import concourse.bass as bass
import concourse.mybir as mybir
from concourse.bass_utils import run_bass_kernel_spmd

F32 = mybir.dt.float32
BF16 = mybir.dt.bfloat16
AF = mybir.ActivationFunctionType
OP = mybir.AluOpType

B, L, D, S = 2, 16384, 1024, 256
NCORE = 8
CHUNKS = 4
TPC = L // CHUNKS
TT = 512
GC, GS, GD = 8, 2, 8

(C_BG, C_BB, C_BC, C_MUR, C_MUK, C_HM, C_XP, C_BM, C_EPS, C_S1024, C_NEGH,
 C_ZERO, C_ONE, C_NBG, C_NEG1) = (0, 2, 4, 12, 20, 28, 44, 52, 53, 54, 55, 56,
                                  57, 58, 60)
NCONST = 61


def build_nc(NT):
    tpc = NT * TT
    nc = bass.Bass("TRN2", target_bir_lowering=False, debug=False)

    xT = nc.dram_tensor("xT", [128, NT * GC * TT], F32,
                        kind="ExternalInput").ap()
    wgT = nc.dram_tensor("wgT", [128, GC * S], F32, kind="ExternalInput").ap()
    wbT = nc.dram_tensor("wbT", [128, GC * S], F32, kind="ExternalInput").ap()
    wcT = nc.dram_tensor("wcT", [128, GS * D], F32, kind="ExternalInput").ap()
    wrT = nc.dram_tensor("wrT", [128, GC * D], F32, kind="ExternalInput").ap()
    wkT = nc.dram_tensor("wkT", [128, GC * D], F32, kind="ExternalInput").ap()
    wvT = nc.dram_tensor("wvT", [128, GC * D], F32, kind="ExternalInput").ap()
    consts = nc.dram_tensor("consts", [128, NCONST], F32, kind="ExternalInput").ap()
    idn_h = nc.dram_tensor("idn_h", [128, 128], F32, kind="ExternalInput").ap()
    onesb_h = nc.dram_tensor("onesb_h", [128, 130], BF16, kind="ExternalInput").ap()
    yT = nc.dram_tensor("yT", [128, NT * GC * TT], F32,
                        kind="ExternalOutput").ap()
    cc_in = nc.dram_tensor("cc_in", [128, 4], F32, kind="Internal").ap()
    warm_in = nc.dram_tensor("warm_in", [128, 4], F32, kind="Internal").ap()
    warm_out2 = nc.dram_tensor("warm_out2", [128 * CHUNKS, 4], F32,
                               kind="Internal").ap()
    cc_out = nc.dram_tensor("cc_out", [128 * CHUNKS, 4], F32, kind="Internal").ap()

    A = nc.alloc_sbuf_tensor
    alpha = A("alpha", [128, GS * tpc], BF16)
    ua = A("ua", [128, GS * tpc], BF16)
    u_wgwb_xr = A("u_wgwb_xr", [128, 2 * GC * S], BF16)
    wg_sb = u_wgwb_xr[:, 0:GC * S]
    wb_sb = u_wgwb_xr[:, GC * S:2 * GC * S]
    wc_sb = A("wc_sb", [128, GS * D], BF16)
    wr_sb = A("wr_sb", [128, GC * D], BF16)
    wk_sb = A("wk_sb", [128, GC * D], BF16)
    wv_sb = A("wv_sb", [128, GC * D], BF16)
    xbt = [A("xbt0", [128, GC * TT], BF16), A("xbt1", [128, GC * TT], BF16)]
    xlnb = [A("xlnb0", [128, GC * (TT + 1)], BF16),
            A("xlnb1", [128, GC * (TT + 1)], BF16)]
    xrb = u_wgwb_xr[:, :]
    xkb = A("xkb", [128, GC * TT], BF16)
    rb = A("rb", [128, GC * TT], BF16)
    kb = A("kb", [128, GC * TT], BF16)
    sqb = A("sqb", [128, GC * TT], BF16)
    hsf = A("hsf", [128, GS * TT], F32)
    hsb = A("hsb", [128, GS * TT], BF16)
    mb1sb = A("mb1sb", [128, TT], BF16)
    ib1sb = A("ib1sb", [128, TT], BF16)
    foldb = A("foldb", [128, 8 * TT], BF16)
    outst = A("outst", [128, GC * TT], F32)
    cns = A("cns", [128, NCONST], F32)
    idn = A("idn", [128, 128], BF16)
    onesb = A("onesb", [128, 130], BF16)
    agin = A("agin", [128, 4], F32)
    agr = A("agr", [128, 4 * CHUNKS], F32)
    hina = A("hina", [128, 8], F32)
    hwk = A("hwk", [128, 8], F32)
    hsel = A("hsel", [128, 2], F32)
    hselbf = A("hselbf", [128, 2], BF16)
    hinit = A("hinit", [128, 16], F32)
    atile = A("atile", [128, 36], F32)
    etile = A("etile", [128, 16], F32)
    ewk = A("ewk", [128, 4], F32)
    bt0 = A("bt0", [128, 8], F32)
    bt1 = A("bt1", [128, 8], F32)
    bt2 = A("bt2", [128, 8], F32)
    btb = A("btb", [128, 2], BF16)
    stf1 = A("stf1", [1, 2 * TT], F32)
    stf2 = A("stf2", [1, 2 * TT], F32)
    mbf1 = A("mbf1", [1, TT], BF16)
    ivf1 = A("ivf1", [1, TT], BF16)
    mbf2 = A("mbf2", [1, TT], BF16)
    ivf2 = A("ivf2", [1, TT], BF16)
    bstf = A("bstf", [1, 8], F32)
    bstb = A("bstb", [1, 2], BF16)
    mubf = A("mubf", [128, 16], BF16)

    P = nc.alloc_psum_tensor
    ps = [P(f"ps{i}", [128, 512], F32) for i in range(8)]

    sems = {}
    def sem(name):
        sems[name] = nc.alloc_semaphore(name)
        return sems[name]
    for n in ("cns_l idn_l wg_l wb_l wc_l wr_l wk_l wv_l xl0 xl1 "
              "mm evp1 scan2 hsbf chain bnd1 bnd2 bnd3 "
              "sq1 sq2 mbf1 mbf2 fin1 fin2 "
              "invbf1 invbf2 mbev1 ibev1 diff xr xk kact rev kev "
              "kvev ap1 ap2 z2 rkv store ccd1 ccc ccd2 ccw ccz").split():
        sem(n)
    sbk = [sem(f"bk{i}") for i in range(8)]
    sem("ones_l")
    sem("sord")
    sem("vord")
    sem("gord")
    semcnt = {}
    def bump(nm):
        semcnt[nm] = semcnt.get(nm, 0) + 1
        return semcnt[nm]

    dep = {}
    plan = {"scalar": [], "vector": [], "gpsimd": []}

    def mkop(engname):
        def op(fn, reads=(), writes=(), inc=None, extra=(), incby=1):
            waits = {}
            for nm, val in extra:
                waits[nm] = max(waits.get(nm, 0), val)
            for b in reads:
                st = dep.get(b)
                if st and st[0]:
                    nm, val = st[0]
                    waits[nm] = max(waits.get(nm, 0), val)
            for b in writes:
                st = dep.get(b)
                if st:
                    if st[0]:
                        nm, val = st[0]
                        waits[nm] = max(waits.get(nm, 0), val)
                    for nm, val in st[1].items():
                        waits[nm] = max(waits.get(nm, 0), val)
            nm = inc or {"scalar": "sord", "vector": "vord",
                         "gpsimd": "gord"}[engname]
            semcnt[nm] = semcnt.get(nm, 0) + incby
            val = semcnt[nm]
            rec = (nm, val)
            plan[engname].append(("op", fn, sorted(waits.items()), nm, incby))
            for b in reads:
                st = dep.setdefault(b, [None, {}])
                st[1][nm] = max(st[1].get(nm, 0), val)
            for b in writes:
                dep[b] = [rec, {}]
            return rec
        return op

    bankrec = {}
    waitrec = {}

    def brel(key, rec):
        g = groups[gidx[key]]
        bankrec.setdefault((g["bank"], g["use"]), []).append(rec)

    groups = []
    gidx = {}
    buse = [0] * 8

    def add_group(kind, tile, bank, waits, emit, key=None, nobank=False):
        g = dict(kind=kind, tile=tile, bank=bank, waits=list(waits),
                 emit=emit, idx=len(groups),
                 use=(None if nobank else buse[bank]))
        if not nobank:
            buse[bank] += 1
        groups.append(g)
        gidx[key or (kind, tile)] = g["idx"]
        return g

    ones_col = onesb[:, 0:1]
    ones_row = onesb[0:1, 2:130]

    def xv(t):
        return xbt[t % 2][:, :].rearrange("p (g w) -> p g w", g=GC)

    def xlv(t):
        return xlnb[t % 2][:, :].rearrange("p (g w) -> p g w", g=GC)[:, :, 1:TT + 1]

    def xpv(t):
        return xlnb[t % 2][:, :].rearrange("p (g w) -> p g w", g=GC)[:, :, 0:TT]

    def bndv(t):
        return xlnb[t % 2][:, :].rearrange("p (g w) -> p g w", g=GC)[:, :, 0:1]

    def lastv(t):
        return xlnb[t % 2][:, :].rearrange("p (g w) -> p g w", g=GC)[:, :, TT:TT + 1]

    for j in range(NT):
        for kind, wsb in (("a", wg_sb), ("u", wb_sb)):
            for gs in range(GS):
                bank = 4 * (j % 2) + (0 if kind == "a" else 2) + gs
                def emit(t, j=j, gs=gs, wsb=wsb, bank=bank):
                    for gc in range(GC):
                        i = t.matmul(ps[bank][:, :],
                                     wsb[:, gc * S + gs * 128: gc * S + (gs + 1) * 128],
                                     xbt[j % 2][:, gc * TT:(gc + 1) * TT],
                                     start=(gc == 0), stop=(gc == GC - 1))
                    return i
                evw = 6 * (j - 2) + (gs + 1 if kind == "a" else 3 + gs)
                add_group(f"p1{kind}", j, bank,
                          [(f"xl{j % 2}", 16 * (j // 2 + 1)),
                           ("wg_l" if kind == "a" else "wb_l", 16)]
                          + ([("evp1", evw)] if j >= 2 else []),
                          emit, key=(f"p1{kind}", j, gs), nobank=True)

    def emit_bnd(t):
        for gd in range(GD):
            for gs in range(GS):
                i = t.matmul(ps[6][:, gd:gd + 1],
                             wc_sb[:, gs * D + gd * 128: gs * D + (gd + 1) * 128],
                             hselbf[:, gs:gs + 1],
                             start=(gs == 0), stop=(gs == GS - 1))
        return i
    add_group("bnd", 0, 6, [("chain", 2), ("wc_l", 16)], emit_bnd)

    def emit_bstata(t):
        return t.matmul(ps[4][0:1, 0:1], ones_col[:, 0:1], btb[:, 0:1],
                        start=True, stop=True)

    def emit_bstatb(t):
        return t.matmul(ps[5][0:1, 0:1], ones_col[:, 0:1], btb[:, 1:2],
                        start=True, stop=True)

    def emit_bbcm(t):
        return t.matmul(ps[6][:, 0:1], ones_row[0:1, :], bstb[0:1, 0:1],
                        start=True, stop=True)

    def emit_bbci(t):
        return t.matmul(ps[7][:, 0:1], ones_row[0:1, :], bstb[0:1, 1:2],
                        start=True, stop=True)

    def add_boundary_stats():
        add_group("bstata", 0, 4, [("bnd1", 2), ("ones_l", 16)], emit_bstata)
        add_group("bstatb", 0, 5, [("bnd1", 2), ("ones_l", 16)], emit_bstatb)
        add_group("bbcm", 0, 6, [("bnd3", 1), ("ones_l", 16)], emit_bbcm)
        add_group("bbci", 0, 7, [("bnd3", 1), ("ones_l", 16)], emit_bbci)

    rot = [0]

    def add_data_group(kind, tile, g, waits, emit_fn):
        bank = rot[0]
        rot[0] = (rot[0] + 1) % 4
        def emit(t, emit_fn=emit_fn, bank=bank):
            return emit_fn(t, bank)
        add_group(kind, tile, bank, waits, emit, key=(kind, tile, g))

    def wc_group(tile, g):
        def emit(t, bank, tile=tile, g=g):
            for gs in range(GS):
                t.matmul(ps[bank][:, :],
                         wc_sb[:, gs * D + g * 128: gs * D + (g + 1) * 128],
                         hsb[:, gs * TT:(gs + 1) * TT],
                         start=(gs == 0), stop=False)
            return t.matmul(ps[bank][:, :], idn[:, :],
                            xbt[tile % 2][:, g * TT:(g + 1) * TT],
                            start=False, stop=True)
        k = NT + tile
        add_data_group("wc", tile, g,
                       [("hsbf", tile + 1), (f"xl{k % 2}", 16 * (k // 2 + 1)),
                        ("wc_l", 16), ("idn_l", 16)], emit)

    def ffn_group(kind, wsb, wsem, ssem, tile, g):
        def emit(t, bank, g=g, wsb=wsb):
            src = {"fr": xrb, "fk": xkb, "fv": sqb}[kind]
            for gc in range(GC):
                i = t.matmul(ps[bank][:, :],
                             wsb[:, gc * D + g * 128: gc * D + (g + 1) * 128],
                             src[:, gc * TT:(gc + 1) * TT],
                             start=(gc == 0), stop=(gc == GC - 1))
            return i
        sv = 8 * (tile + 1) if kind in ("fr", "fk") else (tile + 1)
        add_data_group(kind, tile, g, [(ssem, sv), (wsem, 16)], emit)

    def stats_group(which, tile):
        bank = 4 if which == 1 else 5
        last = (which == 2 and tile == NT - 1)
        folded = (which == 2 and not last)
        def emit(t, which=which, tile=tile, bank=bank, last=last,
                 folded=folded):
            if which == 1 and not folded:
                xb = xbt[tile % 2]
                for k in range(GC):
                    t.matmul(ps[bank][0:1, :], ones_col[:, 0:1],
                             xb[:, k * TT:(k + 1) * TT],
                             start=(k == 0), stop=(k == GC - 1))
            elif folded:
                for k in range(8):
                    row = 0 if k < 4 else 64
                    i = t.matmul(ps[bank][row:row + 1, :], ones_col[:, 0:1],
                                 foldb[:, k * TT:(k + 1) * TT],
                                 start=(k % 4 == 0), stop=(k % 4 == 3))
                return i
            else:
                xl = xlv(tile)
                for k in range(GC):
                    if k == 4:
                        nm, val = waitrec[("z2h", tile, 1)]
                        t.wait_ge(sems[nm], val)
                    t.matmul(ps[bank][0:1, :], ones_col[:, 0:1], xl[:, k],
                             start=(k == 0), stop=(k == GC - 1))
            for k in range(GC):
                if last and k in (0, 4):
                    nm, val = waitrec[("sq2h", tile, k // 4)]
                    t.wait_ge(sems[nm], val)
                i = t.matmul(ps[bank][64:65, :], ones_col[:, 0:1],
                             sqb[:, k * TT:(k + 1) * TT],
                             start=(k == 0), stop=(k == GC - 1))
            return i
        if which == 1:
            waits = [("ones_l", 16), ("REC", ("zev", tile)),
                     ("REC", ("sq1", tile))]
        elif last:
            waits = [("ones_l", 16), ("REC", ("z2h", tile, 0))]
        else:
            waits = [("ones_l", 16), ("REC", (f"fz{which}", tile)),
                     ("REC", (f"fq{which}", tile))]
        add_group(f"st{which}", tile, bank, waits, emit)

    def bc_group(which, mi, tile):
        bank = {"m": 6, "i": 7}[mi]
        row = {(1, "m"): mbf1, (1, "i"): ivf1, (2, "m"): mbf2, (2, "i"): ivf2}[(which, mi)]
        wsem = {"m": f"mbf{which}", "i": f"invbf{which}"}[mi]
        def emit(t, row=row, bank=bank):
            return t.matmul(ps[bank][:, :], ones_row[0:1, :], row[0:1, :],
                            start=True, stop=True)
        add_group(f"bc{which}{mi}", tile, bank,
                  [(wsem, tile + 1), ("ones_l", 16)], emit)

    for g in range(GD):
        wc_group(0, g)
    for g in range(GD):
        wc_group(1, g)
    add_boundary_stats()
    stats_group(1, 0)
    bc_group(1, "m", 0)
    bc_group(1, "i", 0)

    for c in range(NT):
        for g in range(GD):
            ffn_group("fr", wr_sb, "wr_l", "xr", c, g)
        if c >= 1:
            stats_group(2, c - 1)
        for g in range(0, 2):
            ffn_group("fk", wk_sb, "wk_l", "xk", c, g)
        if c >= 1:
            bc_group(2, "m", c - 1)
        for g in range(2, 4):
            ffn_group("fk", wk_sb, "wk_l", "xk", c, g)
        if c >= 1:
            bc_group(2, "i", c - 1)
        if c + 1 < NT:
            stats_group(1, c + 1)
        for g in range(4, 6):
            ffn_group("fk", wk_sb, "wk_l", "xk", c, g)
        if c + 1 < NT:
            bc_group(1, "m", c + 1)
        for g in range(6, 8):
            ffn_group("fk", wk_sb, "wk_l", "xk", c, g)
        if c + 1 < NT:
            bc_group(1, "i", c + 1)
        if c + 2 < NT:
            for g in range(GD):
                wc_group(c + 2, g)
        for g in range(GD):
            ffn_group("fv", wv_sb, "wv_l", "kact", c, g)
    stats_group(2, NT - 1)
    bc_group(2, "m", NT - 1)
    bc_group(2, "i", NT - 1)

    NGROUPS = len(groups)

    def gdone(key):
        return gidx[key] + 1

    XB = lambda p: [("xb", p, g) for g in range(GC)]
    KB = [("kb", g) for g in range(GC)]
    KBLO = [("kb", g) for g in range(4)]
    RB = [("rb", g) for g in range(GC)]
    XRK = lambda wh: [(wh, g) for g in range(GC)]

    sop = mkop("scalar")
    vop = mkop("vector")

    gv = lambda buf: buf[:, :].rearrange("p (g w) -> p g w", g=GC)

    def bcast_g(tile_128xTT):
        return tile_128xTT[:, :].unsqueeze(1).broadcast_to([128, GC, TT])

    zrow = cns[:, C_ZERO:C_ZERO + 1]

    def s_p1_tile(j):
        for gs in range(GS):
            key = ("p1a", j, gs)
            bk = groups[gidx[key]]["bank"]
            extra = [("mm", gdone(key))]
            if j == 0 and gs == 0:
                extra.append(("cns_l", 16))
            sop(lambda e, bk=bk, j=j, gs=gs: e.activation(
                alpha[:, gs * tpc + j * TT: gs * tpc + (j + 1) * TT],
                ps[bk][:, :], AF.Sigmoid,
                bias=cns[:, C_BG + gs:C_BG + gs + 1]),
                writes=[("al", j, gs)], inc="evp1", extra=extra)
        for gs in range(GS):
            key = ("p1u", j, gs)
            bk = groups[gidx[key]]["bank"]
            sop(lambda e, bk=bk, j=j, gs=gs: e.activation(
                ua[:, gs * tpc + j * TT: gs * tpc + (j + 1) * TT],
                ps[bk][:, :], AF.Identity,
                bias=cns[:, C_BB + gs:C_BB + gs + 1]),
                writes=[("uu", j, gs)], inc="evp1",
                extra=[("mm", gdone(key))])
        for gs in range(GS):
            sop(lambda e, j=j, gs=gs: e.activation(
                hsb[:, gs * TT:(gs + 1) * TT],
                alpha[:, gs * tpc + j * TT: gs * tpc + (j + 1) * TT],
                AF.Ln),
                reads=[("al", j, gs)], writes=["hsb"], inc="evp1")

    AEXP = 20

    def s_aexp_tile(j):
        sop(lambda e, j=j: e.activation(
            atile[:, AEXP + 2 * j:AEXP + 2 * j + 2],
            atile[:, 2 * j:2 * j + 2], AF.Exp),
            reads=["atile"], writes=["atile"])

    def s_aexp_total():
        sop(lambda e: e.activation(atile[:, 16:18], atile[:, 16:18], AF.Exp),
            reads=["atile"], writes=["atile"])

    def s_bnd_inv():
        sop(lambda e: e.activation(bstf[0:1, 3:4], bstf[0:1, 2:3], AF.Ln,
                                   bias=cns[0:1, C_EPS:C_EPS + 1]),
            reads=["bstf"], writes=["bstf"])
        sop(lambda e: e.activation(bstf[0:1, 4:5], bstf[0:1, 3:4], AF.Exp,
                                   scale=cns[0:1, C_NEGH:C_NEGH + 1]),
            reads=["bstf"], writes=["bstf"])
        sop(lambda e: e.activation(bstb[0:1, 1:2], bstf[0:1, 4:5], AF.Copy),
            reads=["bstf"], writes=["bstb"], inc="bnd3")

    def hs_cast(tt):
        extra = [("mm", gdone(("wc", tt - 1, GD - 1)))] if tt >= 1 else []
        sop(lambda e: e.activation(hsb[:, :], hsf[:, :], AF.Copy),
            reads=["hsf"], writes=["hsb"], inc="hsbf", extra=extra)

    def z_evs(tt):
        rec = None
        for g in range(GC):
            key = ("wc", tt, g)
            bk = groups[gidx[key]]["bank"]
            extra = [("mm", gdone(("wc", tt, min(g + 1, GC - 1))))] \
                if g % 2 == 0 else []
            rec = sop(lambda e, bk=bk, tt=tt, g=g: e.activation(
                xbt[tt % 2][:, g * TT:(g + 1) * TT], ps[bk][:, :],
                AF.Identity, bias=cns[:, C_BC + g:C_BC + g + 1]),
                writes=[("xb", tt % 2, g)], extra=extra)
            brel(key, rec)
        waitrec[("zev", tt)] = rec

    def sq1(tt):
        extra = []
        if tt >= 2:
            extra = [("mm", gdone(("st2", tt - 2)))]
        elif tt == 1:
            extra = [("mm", gdone(("st1", 0)))]
        rec = sop(lambda e, tt=tt: e.activation(
            sqb[:, :].rearrange("p (g w) -> p g w", g=GC), xv(tt),
            AF.Square), reads=XB(tt % 2), writes=["sqb"], inc="sq1",
            extra=extra)
        waitrec[("sq1", tt)] = rec

    def p2_evs(kind, tt, lo, hi, dst, func, donesem, wkeys):
        for g in range(lo, hi):
            key = (kind, tt, g)
            bk = groups[gidx[key]]["bank"]
            extra = [("mm", gdone((kind, tt, min(g + 1, hi - 1))))] \
                if g % 2 == 0 else []
            rec = sop(lambda e, bk=bk, g=g, dst=dst, func=func: e.activation(
                dst[:, g * TT:(g + 1) * TT], ps[bk][:, :], func),
                writes=[wkeys[g]],
                inc=(donesem if g == GC - 1 else None),
                extra=extra)
            brel(key, rec)

    def r_evs(tt, lo, hi):
        p2_evs("fr", tt, lo, hi, rb, AF.Sigmoid, "rev", RB)

    def k_evs(tt, lo, hi):
        p2_evs("fk", tt, lo, hi, kb, AF.Relu, "kev", KB)

    def kv_evs(tt):
        p2_evs("fv", tt, 0, GC, kb, AF.Copy, "kvev", KB)

    def mbf(which, tt):
        key = (f"st{which}", tt)
        bank = 4 if which == 1 else 5
        row = mbf1 if which == 1 else mbf2
        extra = [("mm", gdone(key))]
        if tt >= 1:
            extra.append(("mm", gdone((f"bc{which}m", tt - 1))))
        sop(lambda e, bank=bank, row=row: e.activation(
            row[0:1, :], ps[bank][0:1, :],
            AF.Identity, scale=cns[0:1, C_S1024:C_S1024 + 1]),
            writes=[f"mbf{which}"], inc=f"mbf{which}", extra=extra)

    def inv_s(which, tt):
        stf = stf1 if which == 1 else stf2
        row = ivf1 if which == 1 else ivf2
        sk = f"stf{which}"
        sop(lambda e, stf=stf: e.activation(
            stf[0:1, 0:TT], stf[0:1, TT:2 * TT], AF.Ln,
            bias=cns[0:1, C_EPS:C_EPS + 1]), reads=[sk], writes=[sk])
        sop(lambda e, stf=stf: e.activation(
            stf[0:1, TT:2 * TT], stf[0:1, 0:TT], AF.Exp,
            scale=cns[0:1, C_NEGH:C_NEGH + 1]), reads=[sk], writes=[sk])
        extra = ([("mm", gdone((f"bc{which}i", tt - 1)))] if tt >= 1 else [])
        sop(lambda e, stf=stf, row=row: e.activation(
            row[0:1, :], stf[0:1, TT:2 * TT], AF.Copy),
            reads=[sk], writes=[f"ivf{which}"], inc=f"invbf{which}",
            extra=extra)

    def mb1_ev(tt):
        key = ("bc1m", tt)
        bk = groups[gidx[key]]["bank"]
        rec = sop(lambda e, bk=bk: e.copy(mb1sb[:, :], ps[bk][:, :]),
                  writes=["mb1"], inc="mbev1",
                  extra=[("mm", gdone(key))])
        brel(key, rec)

    def ib1_ev(tt):
        key = ("bc1i", tt)
        bk = groups[gidx[key]]["bank"]
        rec = sop(lambda e, bk=bk: e.copy(ib1sb[:, :], ps[bk][:, :]),
                  writes=["ib1"], inc="ibev1",
                  extra=[("mm", gdone(key))])
        brel(key, rec)

    def sq2(tt):
        extra = [("mm", gdone(("fv", tt, GD - 1)))]
        rec = sop(lambda e, tt=tt: e.activation(
            sqb[:, :].rearrange("p (g w) -> p g w", g=GC), xlv(tt),
            AF.Square), reads=[("xl", tt % 2)], writes=["sqb"], inc="sq2",
            extra=extra)
        waitrec[("sq2", tt)] = rec

    def v_p1_tile(j):
        for gs in range(GS):
            asl = alpha[:, gs * tpc + j * TT: gs * tpc + (j + 1) * TT]
            usl = ua[:, gs * tpc + j * TT: gs * tpc + (j + 1) * TT]
            vop(lambda e, asl=asl, usl=usl, gs=gs: e.tensor_tensor_scan(
                hsf[:, gs * TT:(gs + 1) * TT], asl, usl, zrow,
                OP.mult, OP.add),
                reads=[("al", j, gs), ("uu", j, gs)], writes=["hsf"])
            vop(lambda e, j=j, gs=gs: e.tensor_copy(
                etile[:, 2 * j + gs:2 * j + gs + 1],
                hsf[:, (gs + 1) * TT - 1:(gs + 1) * TT]),
                reads=["hsf"], writes=["etile"])
        for gs in range(GS):
            vop(lambda e, j=j, gs=gs: e.reduce_sum(
                atile[:, 2 * j + gs:2 * j + gs + 1],
                hsb[:, gs * TT:(gs + 1) * TT], mybir.AxisListType.X),
                reads=["hsb"], writes=["atile"])

    def v_asum_efold():
        vop(lambda e: e.reduce_sum(
            atile[:, 16:18],
            atile[:, 0:2 * NT].rearrange("p (j g) -> p g j", g=2),
            mybir.AxisListType.X), reads=["atile"], writes=["atile"])

    def v_efold_step(j):
        if j == 0:
            vop(lambda e: e.tensor_copy(ewk[:, 0:2], etile[:, 0:2]),
                reads=["etile"], writes=["ewk"])
        else:
            vop(lambda e, j=j: e.tensor_mul(
                ewk[:, 2:4], atile[:, AEXP + 2 * j:AEXP + 2 * j + 2],
                ewk[:, 0:2]),
                reads=["atile", "ewk"], writes=["ewk"])
            vop(lambda e, j=j: e.tensor_add(ewk[:, 0:2], ewk[:, 2:4],
                                            etile[:, 2 * j:2 * j + 2]),
                reads=["etile", "ewk"], writes=["ewk"])

    def v_chain():
        vop(lambda e: e.tensor_copy(agin[:, 0:2], atile[:, 16:18]),
            reads=["atile"], writes=["agin"])
        vop(lambda e: e.tensor_copy(agin[:, 2:4], ewk[:, 0:2]),
            reads=["ewk"], writes=["agin"], inc="chain")
        vop(lambda e: e.tensor_copy(hina[:, 0:2], zrow.broadcast_to([128, 2])),
            writes=["hina"], extra=[("ccd2", 16)])
        vop(lambda e: e.tensor_copy(hina[:, 2:4], agr[:, 2:4]),
            reads=["hina"], writes=["hina"])
        for q in (2, 3):
            pcol = (q - 1) * 4
            vop(lambda e, q=q, pcol=pcol: e.tensor_mul(
                hwk[:, q * 2:q * 2 + 2], agr[:, pcol:pcol + 2],
                hina[:, (q - 1) * 2:(q - 1) * 2 + 2]),
                reads=["hina"], writes=["hwk"])
            vop(lambda e, q=q, pcol=pcol: e.tensor_add(
                hina[:, q * 2:q * 2 + 2], hwk[:, q * 2:q * 2 + 2],
                agr[:, pcol + 2:pcol + 4]),
                reads=["hwk"], writes=["hina"])
        vop(lambda e: e.tensor_mul(hwk[:, :], hina[:, :],
                                   cns[:, C_HM:C_HM + 8]),
            reads=["hina"], writes=["hwk"])
        vop(lambda e: e.tensor_add(hwk[:, 0:4], hwk[:, 0:4], hwk[:, 4:8]),
            reads=["hwk"], writes=["hwk"])
        vop(lambda e: e.tensor_add(hsel[:, :], hwk[:, 0:2], hwk[:, 2:4]),
            reads=["hwk"], writes=["hsel"])
        vop(lambda e: e.tensor_copy(hselbf[:, :], hsel[:, :]),
            reads=["hsel"], writes=["hselbf"], inc="chain")
        vop(lambda e: e.tensor_copy(hinit[:, 0:2], hsel[:, :]),
            reads=["hsel"], writes=["hinit"])

    def hinit_rest():
        for t in range(1, NT):
            vop(lambda e, t=t: e.tensor_mul(
                hwk[:, 0:2], atile[:, AEXP + 2 * (t - 1):AEXP + 2 * t],
                hinit[:, 2 * (t - 1):2 * t]),
                reads=["atile", "hinit"], writes=["hwk"])
            vop(lambda e, t=t: e.tensor_add(
                hinit[:, 2 * t:2 * t + 2], hwk[:, 0:2],
                etile[:, 2 * (t - 1):2 * t]),
                reads=["hwk", "etile"], writes=["hinit"])

    def v_boundary():
        r_ = vop(lambda e: e.tensor_add(bt0[:, :], ps[6][:, 0:8],
                                        cns[:, C_BC:C_BC + 8]),
                 writes=["bt0"], inc="bnd1",
                 extra=[("mm", gdone(("bnd", 0)))])
        brel(("bnd", 0), r_)
        vop(lambda e: e.tensor_add(bt0[:, :], bt0[:, :],
                                   cns[:, C_XP:C_XP + 8]),
            reads=["bt0"], writes=["bt0"])
        vop(lambda e: e.tensor_add(bt1[:, 0:4], bt0[:, 0:4], bt0[:, 4:8]),
            reads=["bt0"], writes=["bt1"])
        vop(lambda e: e.tensor_add(bt1[:, 0:2], bt1[:, 0:2], bt1[:, 2:4]),
            reads=["bt1"], writes=["bt1"])
        vop(lambda e: e.tensor_add(bt1[:, 0:1], bt1[:, 0:1], bt1[:, 1:2]),
            reads=["bt1"], writes=["bt1"])
        vop(lambda e: e.tensor_copy(btb[:, 0:1], bt1[:, 0:1]),
            reads=["bt1"], writes=["btb"])
        vop(lambda e: e.tensor_mul(bt2[:, :], bt0[:, :], bt0[:, :]),
            reads=["bt0"], writes=["bt2"])
        vop(lambda e: e.tensor_add(bt1[:, 0:4], bt2[:, 0:4], bt2[:, 4:8]),
            reads=["bt2"], writes=["bt1"])
        vop(lambda e: e.tensor_add(bt1[:, 0:2], bt1[:, 0:2], bt1[:, 2:4]),
            reads=["bt1"], writes=["bt1"])
        vop(lambda e: e.tensor_add(bt1[:, 0:1], bt1[:, 0:1], bt1[:, 1:2]),
            reads=["bt1"], writes=["bt1"])
        vop(lambda e: e.tensor_copy(btb[:, 1:2], bt1[:, 0:1]),
            reads=["bt1"], writes=["btb"], inc="bnd1")
        vop(lambda e: e.tensor_scalar(bstf[0:1, 0:1], ps[4][0:1, 0:1],
                                      1.0 / D, None, OP.mult),
            writes=["bstf"], extra=[("mm", gdone(("bstatb", 0)))])
        vop(lambda e: e.tensor_copy(bstb[0:1, 0:1], bstf[0:1, 0:1]),
            reads=["bstf"], writes=["bstb"])
        vop(lambda e: e.tensor_mul(bstf[0:1, 1:2], bstf[0:1, 0:1],
                                   bstf[0:1, 0:1]),
            reads=["bstf"], writes=["bstf"])
        r_ = vop(lambda e: e.scalar_tensor_tensor(
            bstf[0:1, 2:3], ps[5][0:1, 0:1], 1.0 / D, bstf[0:1, 1:2],
            OP.mult, OP.subtract),
            reads=["bstf"], writes=["bstf"], inc="bnd2")
        brel(("bstata", 0), r_)
        brel(("bstatb", 0), r_)

    def v_bnd_apply():
        vop(lambda e: e.tensor_sub(bt1[:, :], bt0[:, :],
                                   ps[6][:, 0:1].broadcast_to([128, 8])),
            reads=["bt0"], writes=["bt1"],
            extra=[("mm", gdone(("bbci", 0)))])
        r_ = vop(lambda e: e.tensor_mul(
            bt1[:, :], bt1[:, :], ps[7][:, 0:1].broadcast_to([128, 8])),
            reads=["bt1"], writes=["bt1"], inc="bnd2")
        brel(("bbcm", 0), r_)
        brel(("bbci", 0), r_)
        vop(lambda e: e.tensor_scalar(bt1[:, :], bt1[:, :],
                                      cns[:, C_BM:C_BM + 1], None, OP.mult),
            reads=["bt1"], writes=["bt1"])
        vop(lambda e: e.tensor_copy(bndv(0).squeeze(2), bt1[:, :]),
            reads=["bt1"], writes=[("xl", 0)])

    def scan2(tt):
        for gs in range(GS):
            asl = alpha[:, gs * tpc + tt * TT: gs * tpc + (tt + 1) * TT]
            usl = ua[:, gs * tpc + tt * TT: gs * tpc + (tt + 1) * TT]
            vop(lambda e, asl=asl, usl=usl, gs=gs, tt=tt:
                e.tensor_tensor_scan(hsf[:, gs * TT:(gs + 1) * TT],
                                     asl, usl,
                                     hinit[:, 2 * tt + gs:2 * tt + gs + 1],
                                     OP.mult, OP.add),
                reads=[("al", tt, gs), ("uu", tt, gs), "hinit"],
                writes=["hsf"],
                inc=("scan2" if gs == GS - 1 else None))

    def fin(which, tt):
        key = (f"st{which}", tt)
        sk = f"stf{which}"
        stf = stf1 if which == 1 else stf2
        mrow = mbf1 if which == 1 else mbf2
        bank = 4 if which == 1 else 5
        vop(lambda e, stf=stf, mrow=mrow: e.tensor_mul(
            stf[0:1, 0:TT], mrow[0:1, :], mrow[0:1, :]),
            reads=[f"mbf{which}"], writes=[sk])
        r_ = vop(lambda e, stf=stf, bank=bank: e.scalar_tensor_tensor(
            stf[0:1, TT:2 * TT], ps[bank][64:65, :], 1.0 / D,
            stf[0:1, 0:TT], OP.mult, OP.subtract),
            reads=[sk], writes=[sk], inc=f"fin{which}",
            extra=[("mm", gdone(key))])
        brel(key, r_)

    def apply2(tt):
        keym = ("bc2m", tt)
        keyi = ("bc2i", tt)
        bkm = groups[gidx[keym]]["bank"]
        bki = groups[gidx[keyi]]["bank"]
        for h in range(2):
            gsl = slice(4 * h, 4 * h + 4)
            extram = [("mm", gdone(keym))] if h == 0 else []
            if tt >= 1 and h == 0:
                extram.append(("store", 32 * tt))
            rm = vop(lambda e, tt=tt, bkm=bkm, gsl=gsl: e.tensor_sub(
                gv(outst)[:, gsl], xlv(tt)[:, gsl],
                ps[bkm][:, :].unsqueeze(1).broadcast_to([128, 4, TT])),
                reads=[("xl", tt % 2)], writes=["outst"], extra=extram)
            ri = vop(lambda e, bki=bki, gsl=gsl: e.tensor_mul(
                gv(outst)[:, gsl], gv(outst)[:, gsl],
                ps[bki][:, :].unsqueeze(1).broadcast_to([128, 4, TT])),
                reads=["outst"], writes=["outst"], inc="ap2",
                extra=([("mm", gdone(keyi))] if h == 0 else []))
            if h == 1:
                brel(keym, rm)
                brel(keyi, ri)

    def kact(tt):
        extra = ([("mm", gdone(("fv", tt - 1, GD - 1)))] if tt >= 1 else [])
        if tt + 1 < NT:
            extra.append(("mm", gdone(("st1", tt + 1))))
        else:
            extra.append(("mm", gdone(("st2", tt - 1))))
        vop(lambda e: e.tensor_mul(sqb[:, :], kb[:, :], kb[:, :]),
            reads=KB, writes=["sqb"], inc="kact", extra=extra)

    def apply1(tt):
        for g in range(GC):
            extra = [("mm", gdone(("st2", tt - 2)))] \
                if (tt >= 2 and g == 0) else []
            vop(lambda e, tt=tt, g=g: e.tensor_sub(
                xlv(tt)[:, g], xv(tt)[:, g], mb1sb[:, :]),
                reads=[("xb", tt % 2, g), "mb1"], writes=[("xl", tt % 2)],
                extra=extra)
            vop(lambda e, tt=tt, g=g: e.tensor_mul(
                xlv(tt)[:, g], xlv(tt)[:, g], ib1sb[:, :]),
                reads=["ib1"], writes=[("xl", tt % 2)],
                inc=("ap1" if g == GC - 1 else None))

    def v_diff(tt):
        extra = ([("mm", gdone(("fk", tt - 1, GD - 1)))] if tt >= 1 else [])
        vop(lambda e, tt=tt: e.tensor_sub(gv(xkb), xlv(tt), xpv(tt)),
            reads=[("xl", tt % 2)], writes=XRK("xk"), inc="diff",
            extra=extra)

    def v_adds(tt):
        extra = ([("mm", gdone(("fr", tt - 1, GD - 1)))] if tt >= 1 else [])
        for g in range(GC):
            vop(lambda e, tt=tt, g=g: e.scalar_tensor_tensor(
                gv(xrb)[:, g], gv(xkb)[:, g],
                cns[:, C_MUR + g:C_MUR + g + 1], xpv(tt)[:, g],
                OP.mult, OP.add),
                reads=[("xk", g), ("xl", tt % 2)], writes=[("xr", g)],
                inc="xr", extra=(extra if g == 0 else []))
        for g in range(GC):
            vop(lambda e, tt=tt, g=g: e.scalar_tensor_tensor(
                gv(xkb)[:, g], gv(xkb)[:, g],
                cns[:, C_MUK + g:C_MUK + g + 1], xpv(tt)[:, g],
                OP.mult, OP.add),
                reads=[("xl", tt % 2)], writes=[("xk", g)], inc="xk")

    def bnd_save(tt):
        vop(lambda e, tt=tt: e.tensor_copy(bndv(tt + 1), lastv(tt)),
            reads=[("xl", tt % 2)], writes=[("xl", (tt + 1) % 2)])

    def fold2(tt):
        extra = [("mm", gdone(("st2", tt - 1)))] if tt >= 1 else []
        rz = vop(lambda e, tt=tt: e.tensor_add(
            foldb[:, 0:4 * TT].rearrange("p (g w) -> p g w", g=4),
            xlv(tt)[:, 0:4], xlv(tt)[:, 4:8]),
            reads=[("xl", tt % 2)], writes=["foldb"], extra=extra)
        waitrec[("fz2", tt)] = rz
        rq = vop(lambda e: e.tensor_add(
            foldb[:, 4 * TT:8 * TT].rearrange("p (g w) -> p g w", g=4),
            gv(sqb)[:, 0:4], gv(sqb)[:, 4:8]),
            reads=["sqb"], writes=["foldb"])
        waitrec[("fq2", tt)] = rq

    def rkv_z2(tt):
        vop(lambda e: e.tensor_mul(kb[:, :], rb[:, :], kb[:, :]),
            reads=RB + KB, writes=KB, inc="rkv")
        rec = vop(lambda e, tt=tt: e.tensor_add(xlv(tt), gv(kb), xlv(tt)),
                  reads=KB + [("xl", tt % 2)], writes=[("xl", tt % 2)],
                  inc="z2")
        waitrec[("z2", tt)] = rec

    def rkv_z2_sq2_tail(tt):
        for h in range(2):
            kh = [("kb", g) for g in range(4 * h, 4 * h + 4)]
            rh = [("rb", g) for g in range(4 * h, 4 * h + 4)]
            sl = slice(4 * h * TT, (4 * h + 4) * TT)
            gsl = slice(4 * h, 4 * h + 4)
            vop(lambda e, sl=sl: e.tensor_mul(kb[:, sl], rb[:, sl],
                                              kb[:, sl]),
                reads=rh + kh, writes=kh, inc="rkv")
            rec = vop(lambda e, tt=tt, gsl=gsl: e.tensor_add(
                xlv(tt)[:, gsl], gv(kb)[:, gsl], xlv(tt)[:, gsl]),
                reads=kh + [("xl", tt % 2)], writes=[("xl", tt % 2)],
                inc="z2")
            waitrec[("z2h", tt, h)] = rec
            srec = sop(lambda e, tt=tt, gsl=gsl, sl=sl: e.activation(
                sqb[:, sl].rearrange("p (g w) -> p g w", g=4), xlv(tt)[:, gsl],
                AF.Square), reads=[("xl", tt % 2)], writes=["sqb"], inc="sq2",
                extra=([("mm", gdone(("fv", tt, GD - 1)))] if h == 0 else []))
            waitrec[("sq2h", tt, h)] = srec
        waitrec[("z2", tt)] = rec
        waitrec[("sq2", tt)] = srec

    for j in range(NT):
        s_p1_tile(j)
        v_p1_tile(j)
        s_aexp_tile(j)
        v_efold_step(j)
    v_asum_efold()
    s_aexp_total()
    v_chain()
    scan2(0)
    hs_cast(0)
    z_evs(0)
    hinit_rest()
    scan2(1)
    hs_cast(1)
    sq1(0)
    z_evs(1)
    v_boundary()
    s_bnd_inv()
    v_bnd_apply()
    mbf(1, 0)
    fin(1, 0)
    inv_s(1, 0)
    mb1_ev(0)
    ib1_ev(0)
    apply1(0)
    v_diff(0)
    v_adds(0)
    bnd_save(0)
    for c in range(NT):
        if c + 2 < NT:
            scan2(c + 2)
        r_evs(c, 0, 4)
        if c >= 1:
            sq2(c - 1)
            fold2(c - 1)
        if c + 2 < NT:
            hs_cast(c + 2)
        r_evs(c, 4, 8)
        if c >= 1:
            mbf(2, c - 1)
            fin(2, c - 1)
        if c + 1 < NT:
            sq1(c + 1)
        if c >= 1:
            inv_s(2, c - 1)
            apply2(c - 1)
        k_evs(c, 0, 4)
        if c + 1 < NT:
            mbf(1, c + 1)
            fin(1, c + 1)
            inv_s(1, c + 1)
        k_evs(c, 4, 8)
        kact(c)
        if c + 1 < NT:
            mb1_ev(c + 1)
            ib1_ev(c + 1)
        if c + 2 < NT:
            z_evs(c + 2)
        if c + 1 < NT:
            apply1(c + 1)
            v_diff(c + 1)
            v_adds(c + 1)
            bnd_save(c + 1)
        kv_evs(c)
        if c < NT - 1:
            rkv_z2(c)
        else:
            rkv_z2_sq2_tail(c)
    mbf(2, NT - 1)
    fin(2, NT - 1)
    inv_s(2, NT - 1)
    apply2(NT - 1)


    with nc.Block() as block:

        @block.tensor
        def _(t):
            for g in groups:
                for s, v in g["waits"]:
                    if s == "REC":
                        nm, val = waitrec[v]
                        t.wait_ge(sems[nm], val)
                    else:
                        t.wait_ge(sems[s], v)
                if g["use"] is not None and g["use"] >= 1:
                    for nm, val in bankrec[(g["bank"], g["use"] - 1)]:
                        t.wait_ge(sems[nm], val)
                i = g["emit"](t)
                i.then_inc(sems["mm"], 1)

        def replay(eng, lane):
            for item in plan[lane]:
                _, fn, waits, nm, incby = item
                for wn, wv in waits:
                    eng.wait_ge(sems[wn], wv)
                fn(eng).then_inc(sems[nm], incby)

        @block.scalar
        def _(s):
            replay(s, "scalar")

        @block.vector
        def _(v):
            replay(v, "vector")

        @block.gpsimd
        def _(g):
            def dma(out, in_, semname):
                g.dma_start(out=out, in_=in_).then_inc(sems[semname], 16)

            def xload(k):
                tt = k % NT
                if k >= 2:
                    if k < NT + 2:
                        g.wait_ge(sems["mm"], gdone(("p1u", (k - 2) % NT, GS - 1)))
                    else:
                        g.wait_ge(sems["ap1"], (k - NT - 2) + 1)
                dma(xbt[k % 2][:, :],
                    xT[:, tt * GC * TT:(tt + 1) * GC * TT], f"xl{k % 2}")

            g.dma_start(out=warm_in, in_=consts[0:128, 0:4]
                        ).then_inc(sems["ccz"], 16)
            g.wait_ge(sems["ccz"], 16)
            g.collective_compute(
                "AllGather", OP.bypass,
                replica_groups=[[0, 1, 2, 3], [4, 5, 6, 7]],
                ins=[warm_in.opt()], outs=[cc_out.opt()],
            ).then_inc(sems["ccw"], 1)
            g.collective_compute(
                "AllGather", OP.bypass,
                replica_groups=[[0, 1, 2, 3], [4, 5, 6, 7]],
                ins=[warm_in.opt()], outs=[warm_out2.opt()],
            ).then_inc(sems["ccw"], 1)
            dma(wg_sb[:, :], wgT, "wg_l")
            xload(0)
            dma(wb_sb[:, :], wbT, "wb_l")
            dma(cns[:, :], consts, "cns_l")
            xload(1)
            if NT > 2:
                xload(2)
            dma(idn[:, :], idn_h, "idn_l")
            dma(onesb[:, :], onesb_h, "ones_l")
            for j in range(3, NT):
                xload(j)
            dma(wc_sb[:, :], wcT, "wc_l")
            xload(NT)
            xload(NT + 1)

            g.wait_ge(sems["ccw"], 2)
            g.wait_ge(sems["ccd1"], 16)
            g.collective_compute(
                "AllGather", OP.bypass,
                replica_groups=[[0, 1, 2, 3], [4, 5, 6, 7]],
                ins=[cc_in.opt()], outs=[cc_out.opt()],
            ).then_inc(sems["ccc"], 1)
            g.wait_ge(sems["ccc"], 1)

            dma(wr_sb[:, :], wrT, "wr_l")
            dma(wk_sb[:, :], wkT, "wk_l")
            dma(wv_sb[:, :], wvT, "wv_l")

            for k in range(NT + 2, 2 * NT):
                xload(k)

        @block.sync
        def _(sy):
            sy.wait_ge(sems["ccw"], 2)
            sy.wait_ge(sems["chain"], 1)
            sy.dma_start(out=cc_in, in_=agin[:, :]).then_inc(sems["ccd1"], 16)
            sy.wait_ge(sems["ccc"], 1)
            sy.dma_start(out=agr[:, :].rearrange("p (r c) -> p r c", r=CHUNKS),
                         in_=cc_out.rearrange("(r p) c -> p r c", p=128)
                         ).then_inc(sems["ccd2"], 16)
            for tt in range(NT):
                for h in range(2):
                    sy.wait_ge(sems["ap2"], 2 * tt + h + 1)
                    sy.dma_start(
                        out=yT[:, (tt * GC + 4 * h) * TT:
                                (tt * GC + 4 * h + 4) * TT],
                        in_=outst[:, 4 * h * TT:(4 * h + 4) * TT]
                    ).then_inc(sems["store"], 16)
            sy.wait_ge(sems["store"], 32 * NT)

    return nc


def _prep_inputs(inputs, NT):
    x = np.ascontiguousarray(np.asarray(inputs["L0_attn_x"], dtype=np.float32))
    Wg = np.asarray(inputs["L0_attn_Wg"], dtype=np.float32)
    bg = np.asarray(inputs["L0_attn_bg"], dtype=np.float32)
    WB = np.asarray(inputs["L0_attn_WB"], dtype=np.float32)
    bB = np.asarray(inputs["L0_attn_bB"], dtype=np.float32)
    WC = np.asarray(inputs["L0_attn_WC"], dtype=np.float32)
    bC = np.asarray(inputs["L0_attn_bC"], dtype=np.float32)
    Wr = np.asarray(inputs["L0_ffn_W_r"], dtype=np.float32)
    Wk = np.asarray(inputs["L0_ffn_W_k"], dtype=np.float32)
    Wv = np.asarray(inputs["L0_ffn_W_v"], dtype=np.float32)
    mur = np.asarray(inputs["L0_ffn_mu_r"], dtype=np.float32)
    muk = np.asarray(inputs["L0_ffn_mu_k"], dtype=np.float32)

    tpc = NT * TT

    def lhst(wT, ng):
        return np.ascontiguousarray(
            wT.reshape(ng, 128, wT.shape[1]).transpose(1, 0, 2)
            .reshape(128, ng * wT.shape[1]))

    wgT = lhst(np.ascontiguousarray(Wg.T), GC)
    wbT = lhst(np.ascontiguousarray(WB.T), GC)
    wcT = lhst(np.ascontiguousarray(WC.T), GS)
    wrT = lhst(np.ascontiguousarray(Wr.T), GC)
    wkT = lhst(np.ascontiguousarray(Wk.T), GC)
    wvT = lhst(np.ascontiguousarray(Wv.T), GC)
    idn = np.eye(128, dtype=np.float32)

    in_maps = []
    for c in range(NCORE):
        b, q = c // CHUNKS, c % CHUNKS
        t0 = q * tpc
        xsh = np.ascontiguousarray(
            x[b, t0:t0 + tpc, :].reshape(NT, TT, GC, 128)
            .transpose(3, 0, 2, 1).reshape(128, NT * GC * TT))
        cns = np.zeros((128, NCONST), np.float32)
        cns[:, C_BG:C_BG + 2] = bg.reshape(2, 128).T
        cns[:, C_BB:C_BB + 2] = bB.reshape(2, 128).T
        cns[:, C_BC:C_BC + 8] = bC.reshape(8, 128).T
        cns[:, C_MUR:C_MUR + 8] = mur.reshape(8, 128).T
        cns[:, C_MUK:C_MUK + 8] = muk.reshape(8, 128).T
        hm = np.zeros((128, 16), np.float32)
        hm[:, 2 * (c % CHUNKS):2 * (c % CHUNKS) + 2] = 1.0
        cns[:, C_HM:C_HM + 16] = hm
        if q > 0:
            cns[:, C_XP:C_XP + 8] = x[b, t0 - 1, :].reshape(8, 128).T
            cns[:, C_BM] = 1.0
        cns[:, C_EPS] = 1e-5
        cns[:, C_S1024] = 1.0 / D
        cns[:, C_NEGH] = -0.5
        cns[:, C_ONE] = 1.0
        cns[:, C_NBG:C_NBG + 2] = -bg.reshape(2, 128).T
        cns[:, C_NEG1] = -1.0
        import ml_dtypes
        in_maps.append(dict(
            xT=xsh, wgT=wgT, wbT=wbT, wcT=wcT, wrT=wrT, wkT=wkT, wvT=wvT,
            consts=cns, idn_h=idn,
            onesb_h=np.ones((128, 130), dtype=ml_dtypes.bfloat16)))
    return in_maps


def _out_chunk(y2, NT):
    return (np.asarray(y2).reshape(128, NT, GC, TT)
            .transpose(1, 3, 2, 0).reshape(NT * TT, D))


def kernel(**inputs):
    NT = TPC // TT
    nc = build_nc(NT)
    in_maps = _prep_inputs(inputs, NT)
    res = run_bass_kernel_spmd(nc, in_maps, core_ids=list(range(NCORE)))
    out = np.empty((B, L, D), np.float32)
    tpc = NT * TT
    for c in range(NCORE):
        b, q = c // CHUNKS, c % CHUNKS
        out[b, q * tpc:(q + 1) * tpc, :] = _out_chunk(res.results[c]["yT"], NT)
    return out


# version 24
# speedup vs baseline: 1.0182x; 1.0182x over previous
import numpy as np
import concourse.bass as bass
import concourse.mybir as mybir
from concourse.bass_utils import run_bass_kernel_spmd

F32 = mybir.dt.float32
BF16 = mybir.dt.bfloat16
AF = mybir.ActivationFunctionType
OP = mybir.AluOpType

B, L, D, S = 2, 16384, 1024, 256
NCORE = 8
CHUNKS = 4
TPC = L // CHUNKS
TT = 512
GC, GS, GD = 8, 2, 8

(C_BG, C_BB, C_BC, C_MUR, C_MUK, C_HM, C_XP, C_BM, C_EPS, C_S1024, C_NEGH,
 C_ZERO, C_ONE, C_NBG, C_NEG1) = (0, 2, 4, 12, 20, 28, 44, 52, 53, 54, 55, 56,
                                  57, 58, 60)
NCONST = 61


def build_nc(NT):
    tpc = NT * TT
    nc = bass.Bass("TRN2", target_bir_lowering=False, debug=False)

    xT = nc.dram_tensor("xT", [128, NT * GC * TT], F32,
                        kind="ExternalInput").ap()
    wgT = nc.dram_tensor("wgT", [128, GC * S], F32, kind="ExternalInput").ap()
    wbT = nc.dram_tensor("wbT", [128, GC * S], F32, kind="ExternalInput").ap()
    wcT = nc.dram_tensor("wcT", [128, GS * D], F32, kind="ExternalInput").ap()
    wrT = nc.dram_tensor("wrT", [128, GC * D], F32, kind="ExternalInput").ap()
    wkT = nc.dram_tensor("wkT", [128, GC * D], F32, kind="ExternalInput").ap()
    wvT = nc.dram_tensor("wvT", [128, GC * D], F32, kind="ExternalInput").ap()
    consts = nc.dram_tensor("consts", [128, NCONST], F32, kind="ExternalInput").ap()
    idn_h = nc.dram_tensor("idn_h", [128, 128], F32, kind="ExternalInput").ap()
    onesb_h = nc.dram_tensor("onesb_h", [128, 130], BF16, kind="ExternalInput").ap()
    yT = nc.dram_tensor("yT", [128, NT * GC * TT], F32,
                        kind="ExternalOutput").ap()
    cc_in = nc.dram_tensor("cc_in", [128, 4], F32, kind="Internal").ap()
    warm_in = nc.dram_tensor("warm_in", [128, 4], F32, kind="Internal").ap()
    warm_out2 = nc.dram_tensor("warm_out2", [128 * CHUNKS, 4], F32,
                               kind="Internal").ap()
    cc_out = nc.dram_tensor("cc_out", [128 * CHUNKS, 4], F32, kind="Internal").ap()

    A = nc.alloc_sbuf_tensor
    alpha = A("alpha", [128, GS * tpc], BF16)
    ua = A("ua", [128, GS * tpc], BF16)
    u_wgwb_xr = A("u_wgwb_xr", [128, 2 * GC * S], BF16)
    wg_sb = u_wgwb_xr[:, 0:GC * S]
    wb_sb = u_wgwb_xr[:, GC * S:2 * GC * S]
    wc_sb = A("wc_sb", [128, GS * D], BF16)
    wr_sb = A("wr_sb", [128, GC * D], BF16)
    wk_sb = A("wk_sb", [128, GC * D], BF16)
    wv_sb = A("wv_sb", [128, GC * D], BF16)
    xbt = [A("xbt0", [128, GC * TT], BF16), A("xbt1", [128, GC * TT], BF16)]
    xlnb = [A("xlnb0", [128, GC * (TT + 1)], BF16),
            A("xlnb1", [128, GC * (TT + 1)], BF16)]
    xrb = u_wgwb_xr[:, :]
    xkb = A("xkb", [128, GC * TT], BF16)
    rb = A("rb", [128, GC * TT], BF16)
    kb = A("kb", [128, GC * TT], BF16)
    sqb = A("sqb", [128, GC * TT], BF16)
    hsf = A("hsf", [128, GS * TT], F32)
    hsb = [A("hsb0", [128, GS * TT], BF16), A("hsb1", [128, GS * TT], BF16)]
    mb1sb = A("mb1sb", [128, TT], BF16)
    ib1sb = A("ib1sb", [128, TT], BF16)
    foldb = A("foldb", [128, 8 * TT], BF16)
    outst = A("outst", [128, GC * TT], F32)
    cns = A("cns", [128, NCONST], F32)
    idn = A("idn", [128, 128], BF16)
    onesb = A("onesb", [128, 130], BF16)
    agin = A("agin", [128, 4], F32)
    agr = A("agr", [128, 4 * CHUNKS], F32)
    hina = A("hina", [128, 8], F32)
    hwk = A("hwk", [128, 8], F32)
    hsel = A("hsel", [128, 2], F32)
    hselbf = A("hselbf", [128, 2], BF16)
    hinit = A("hinit", [128, 16], F32)
    atile = A("atile", [128, 36], F32)
    etile = A("etile", [128, 16], F32)
    ewk = A("ewk", [128, 4], F32)
    bt0 = A("bt0", [128, 8], F32)
    bt1 = A("bt1", [128, 8], F32)
    bt2 = A("bt2", [128, 8], F32)
    btb = A("btb", [128, 2], BF16)
    stf1 = A("stf1", [1, 2 * TT], F32)
    stf2 = A("stf2", [1, 2 * TT], F32)
    mbf1 = A("mbf1", [1, TT], BF16)
    ivf1 = A("ivf1", [1, TT], BF16)
    mbf2 = A("mbf2", [1, TT], BF16)
    ivf2 = A("ivf2", [1, TT], BF16)
    bstf = A("bstf", [1, 8], F32)
    bstb = A("bstb", [1, 2], BF16)
    mubf = A("mubf", [128, 16], BF16)

    P = nc.alloc_psum_tensor
    ps = [P(f"ps{i}", [128, 512], F32) for i in range(8)]

    sems = {}
    def sem(name):
        sems[name] = nc.alloc_semaphore(name)
        return sems[name]
    for n in ("cns_l idn_l wg_l wb_l wc_l wr_l wk_l wv_l xl0 xl1 "
              "mm evp1 scan2 hsbf chain bnd1 bnd2 bnd3 "
              "sq1 sq2 mbf1 mbf2 fin1 fin2 "
              "invbf1 invbf2 mbev1 ibev1 diff xr xk kact rev kev "
              "kvev ap1 ap2 z2 rkv store ccd1 ccc ccd2 ccw ccz").split():
        sem(n)
    sbk = [sem(f"bk{i}") for i in range(8)]
    sem("ones_l")
    sem("sord")
    sem("vord")
    sem("gord")
    semcnt = {}
    def bump(nm):
        semcnt[nm] = semcnt.get(nm, 0) + 1
        return semcnt[nm]

    dep = {}
    plan = {"scalar": [], "vector": [], "gpsimd": []}

    def mkop(engname):
        def op(fn, reads=(), writes=(), inc=None, extra=(), incby=1):
            waits = {}
            for nm, val in extra:
                waits[nm] = max(waits.get(nm, 0), val)
            for b in reads:
                st = dep.get(b)
                if st and st[0]:
                    nm, val = st[0]
                    waits[nm] = max(waits.get(nm, 0), val)
            for b in writes:
                st = dep.get(b)
                if st:
                    if st[0]:
                        nm, val = st[0]
                        waits[nm] = max(waits.get(nm, 0), val)
                    for nm, val in st[1].items():
                        waits[nm] = max(waits.get(nm, 0), val)
            nm = inc or {"scalar": "sord", "vector": "vord",
                         "gpsimd": "gord"}[engname]
            semcnt[nm] = semcnt.get(nm, 0) + incby
            val = semcnt[nm]
            rec = (nm, val)
            plan[engname].append(("op", fn, sorted(waits.items()), nm, incby))
            for b in reads:
                st = dep.setdefault(b, [None, {}])
                st[1][nm] = max(st[1].get(nm, 0), val)
            for b in writes:
                dep[b] = [rec, {}]
            return rec
        return op

    bankrec = {}
    waitrec = {}

    def brel(key, rec):
        g = groups[gidx[key]]
        bankrec.setdefault((g["bank"], g["use"]), []).append(rec)

    groups = []
    gidx = {}
    buse = [0] * 8

    def add_group(kind, tile, bank, waits, emit, key=None, nobank=False):
        g = dict(kind=kind, tile=tile, bank=bank, waits=list(waits),
                 emit=emit, idx=len(groups),
                 use=(None if nobank else buse[bank]))
        if not nobank:
            buse[bank] += 1
        groups.append(g)
        gidx[key or (kind, tile)] = g["idx"]
        return g

    ones_col = onesb[:, 0:1]
    ones_row = onesb[0:1, 2:130]

    def xv(t):
        return xbt[t % 2][:, :].rearrange("p (g w) -> p g w", g=GC)

    def xlv(t):
        return xlnb[t % 2][:, :].rearrange("p (g w) -> p g w", g=GC)[:, :, 1:TT + 1]

    def xpv(t):
        return xlnb[t % 2][:, :].rearrange("p (g w) -> p g w", g=GC)[:, :, 0:TT]

    def bndv(t):
        return xlnb[t % 2][:, :].rearrange("p (g w) -> p g w", g=GC)[:, :, 0:1]

    def lastv(t):
        return xlnb[t % 2][:, :].rearrange("p (g w) -> p g w", g=GC)[:, :, TT:TT + 1]

    for j in range(NT):
        for kind, wsb in (("a", wg_sb), ("u", wb_sb)):
            for gs in range(GS):
                bank = 4 * (j % 2) + (0 if kind == "a" else 2) + gs
                def emit(t, j=j, gs=gs, wsb=wsb, bank=bank):
                    for gc in range(GC):
                        i = t.matmul(ps[bank][:, :],
                                     wsb[:, gc * S + gs * 128: gc * S + (gs + 1) * 128],
                                     xbt[j % 2][:, gc * TT:(gc + 1) * TT],
                                     start=(gc == 0), stop=(gc == GC - 1))
                    return i
                evw = 6 * (j - 2) + (gs + 1 if kind == "a" else 3 + gs)
                add_group(f"p1{kind}", j, bank,
                          [(f"xl{j % 2}", 16 * (j // 2 + 1)),
                           ("wg_l" if kind == "a" else "wb_l", 16)]
                          + ([("evp1", evw)] if j >= 2 else []),
                          emit, key=(f"p1{kind}", j, gs), nobank=True)

    def emit_bnd(t):
        for gd in range(GD):
            for gs in range(GS):
                i = t.matmul(ps[6][:, gd:gd + 1],
                             wc_sb[:, gs * D + gd * 128: gs * D + (gd + 1) * 128],
                             hselbf[:, gs:gs + 1],
                             start=(gs == 0), stop=(gs == GS - 1))
        return i
    add_group("bnd", 0, 6, [("chain", 2), ("wc_l", 16)], emit_bnd)

    def emit_bstata(t):
        return t.matmul(ps[4][0:1, 0:1], ones_col[:, 0:1], btb[:, 0:1],
                        start=True, stop=True)

    def emit_bstatb(t):
        return t.matmul(ps[5][0:1, 0:1], ones_col[:, 0:1], btb[:, 1:2],
                        start=True, stop=True)

    def emit_bbcm(t):
        return t.matmul(ps[6][:, 0:1], ones_row[0:1, :], bstb[0:1, 0:1],
                        start=True, stop=True)

    def emit_bbci(t):
        return t.matmul(ps[7][:, 0:1], ones_row[0:1, :], bstb[0:1, 1:2],
                        start=True, stop=True)

    def add_boundary_stats():
        add_group("bstata", 0, 4, [("bnd1", 2), ("ones_l", 16)], emit_bstata)
        add_group("bstatb", 0, 5, [("bnd1", 2), ("ones_l", 16)], emit_bstatb)
        add_group("bbcm", 0, 6, [("bnd3", 1), ("ones_l", 16)], emit_bbcm)
        add_group("bbci", 0, 7, [("bnd3", 1), ("ones_l", 16)], emit_bbci)

    rot = [0]

    def add_data_group(kind, tile, g, waits, emit_fn):
        bank = rot[0]
        rot[0] = (rot[0] + 1) % 4
        def emit(t, emit_fn=emit_fn, bank=bank):
            return emit_fn(t, bank)
        add_group(kind, tile, bank, waits, emit, key=(kind, tile, g))

    def wc_group(tile, g):
        def emit(t, bank, tile=tile, g=g):
            for gs in range(GS):
                t.matmul(ps[bank][:, :],
                         wc_sb[:, gs * D + g * 128: gs * D + (g + 1) * 128],
                         hsb[tile % 2][:, gs * TT:(gs + 1) * TT],
                         start=(gs == 0), stop=False)
            return t.matmul(ps[bank][:, :], idn[:, :],
                            xbt[tile % 2][:, g * TT:(g + 1) * TT],
                            start=False, stop=True)
        k = NT + tile
        add_data_group("wc", tile, g,
                       [("hsbf", tile + 1), (f"xl{k % 2}", 16 * (k // 2 + 1)),
                        ("wc_l", 16), ("idn_l", 16)], emit)

    def ffn_group(kind, wsb, wsem, ssem, tile, g):
        def emit(t, bank, g=g, wsb=wsb):
            src = {"fr": xrb, "fk": xkb, "fv": sqb}[kind]
            for gc in range(GC):
                i = t.matmul(ps[bank][:, :],
                             wsb[:, gc * D + g * 128: gc * D + (g + 1) * 128],
                             src[:, gc * TT:(gc + 1) * TT],
                             start=(gc == 0), stop=(gc == GC - 1))
            return i
        sv = 8 * (tile + 1) if kind in ("fr", "fk") else (tile + 1)
        add_data_group(kind, tile, g, [(ssem, sv), (wsem, 16)], emit)

    def stats_group(which, tile):
        bank = 4 if which == 1 else 5
        last = (which == 2 and tile == NT - 1)
        folded = (which == 2 and not last)
        def emit(t, which=which, tile=tile, bank=bank, last=last,
                 folded=folded):
            if which == 1 and not folded:
                xb = xbt[tile % 2]
                for k in range(GC):
                    t.matmul(ps[bank][0:1, :], ones_col[:, 0:1],
                             xb[:, k * TT:(k + 1) * TT],
                             start=(k == 0), stop=(k == GC - 1))
            elif folded:
                for k in range(8):
                    row = 0 if k < 4 else 64
                    i = t.matmul(ps[bank][row:row + 1, :], ones_col[:, 0:1],
                                 foldb[:, k * TT:(k + 1) * TT],
                                 start=(k % 4 == 0), stop=(k % 4 == 3))
                return i
            else:
                xl = xlv(tile)
                for k in range(GC):
                    if k == 4:
                        nm, val = waitrec[("z2h", tile, 1)]
                        t.wait_ge(sems[nm], val)
                    t.matmul(ps[bank][0:1, :], ones_col[:, 0:1], xl[:, k],
                             start=(k == 0), stop=(k == GC - 1))
            for k in range(GC):
                if last and k in (0, 4):
                    nm, val = waitrec[("sq2h", tile, k // 4)]
                    t.wait_ge(sems[nm], val)
                i = t.matmul(ps[bank][64:65, :], ones_col[:, 0:1],
                             sqb[:, k * TT:(k + 1) * TT],
                             start=(k == 0), stop=(k == GC - 1))
            return i
        if which == 1:
            waits = [("ones_l", 16), ("REC", ("zev", tile)),
                     ("REC", ("sq1", tile))]
        elif last:
            waits = [("ones_l", 16), ("REC", ("z2h", tile, 0))]
        else:
            waits = [("ones_l", 16), ("REC", (f"fz{which}", tile)),
                     ("REC", (f"fq{which}", tile))]
        add_group(f"st{which}", tile, bank, waits, emit)

    def bc_group(which, mi, tile):
        bank = {"m": 6, "i": 7}[mi]
        row = {(1, "m"): mbf1, (1, "i"): ivf1, (2, "m"): mbf2, (2, "i"): ivf2}[(which, mi)]
        wsem = {"m": f"mbf{which}", "i": f"invbf{which}"}[mi]
        def emit(t, row=row, bank=bank):
            return t.matmul(ps[bank][:, :], ones_row[0:1, :], row[0:1, :],
                            start=True, stop=True)
        add_group(f"bc{which}{mi}", tile, bank,
                  [(wsem, tile + 1), ("ones_l", 16)], emit)

    for g in range(GD):
        wc_group(0, g)
    for g in range(GD):
        wc_group(1, g)
    add_boundary_stats()
    stats_group(1, 0)
    bc_group(1, "m", 0)
    bc_group(1, "i", 0)

    for c in range(NT):
        for g in range(GD):
            ffn_group("fr", wr_sb, "wr_l", "xr", c, g)
        if c >= 1:
            stats_group(2, c - 1)
        for g in range(0, 2):
            ffn_group("fk", wk_sb, "wk_l", "xk", c, g)
        if c >= 1:
            bc_group(2, "m", c - 1)
        for g in range(2, 4):
            ffn_group("fk", wk_sb, "wk_l", "xk", c, g)
        if c >= 1:
            bc_group(2, "i", c - 1)
        if c + 1 < NT:
            stats_group(1, c + 1)
        for g in range(4, 6):
            ffn_group("fk", wk_sb, "wk_l", "xk", c, g)
        if c + 1 < NT:
            bc_group(1, "m", c + 1)
        for g in range(6, 8):
            ffn_group("fk", wk_sb, "wk_l", "xk", c, g)
        if c + 1 < NT:
            bc_group(1, "i", c + 1)
        if c + 2 < NT:
            for g in range(GD):
                wc_group(c + 2, g)
        for g in range(GD):
            ffn_group("fv", wv_sb, "wv_l", "kact", c, g)
    stats_group(2, NT - 1)
    bc_group(2, "m", NT - 1)
    bc_group(2, "i", NT - 1)

    NGROUPS = len(groups)

    def gdone(key):
        return gidx[key] + 1

    XB = lambda p: [("xb", p, g) for g in range(GC)]
    KB = [("kb", g) for g in range(GC)]
    KBLO = [("kb", g) for g in range(4)]
    RB = [("rb", g) for g in range(GC)]
    XRK = lambda wh: [(wh, g) for g in range(GC)]

    sop = mkop("scalar")
    vop = mkop("vector")

    gv = lambda buf: buf[:, :].rearrange("p (g w) -> p g w", g=GC)

    def bcast_g(tile_128xTT):
        return tile_128xTT[:, :].unsqueeze(1).broadcast_to([128, GC, TT])

    zrow = cns[:, C_ZERO:C_ZERO + 1]

    def s_p1_tile(j):
        for gs in range(GS):
            key = ("p1a", j, gs)
            bk = groups[gidx[key]]["bank"]
            extra = [("mm", gdone(key))]
            if j == 0 and gs == 0:
                extra.append(("cns_l", 16))
            sop(lambda e, bk=bk, j=j, gs=gs: e.activation(
                alpha[:, gs * tpc + j * TT: gs * tpc + (j + 1) * TT],
                ps[bk][:, :], AF.Sigmoid,
                bias=cns[:, C_BG + gs:C_BG + gs + 1]),
                writes=[("al", j, gs)], inc="evp1", extra=extra)
        for gs in range(GS):
            key = ("p1u", j, gs)
            bk = groups[gidx[key]]["bank"]
            sop(lambda e, bk=bk, j=j, gs=gs: e.activation(
                ua[:, gs * tpc + j * TT: gs * tpc + (j + 1) * TT],
                ps[bk][:, :], AF.Identity,
                bias=cns[:, C_BB + gs:C_BB + gs + 1]),
                writes=[("uu", j, gs)], inc="evp1",
                extra=[("mm", gdone(key))])
        for gs in range(GS):
            sop(lambda e, j=j, gs=gs: e.activation(
                hsb[0][:, gs * TT:(gs + 1) * TT],
                alpha[:, gs * tpc + j * TT: gs * tpc + (j + 1) * TT],
                AF.Ln),
                reads=[("al", j, gs)], writes=[("hsb", 0)], inc="evp1")

    AEXP = 20

    def s_aexp_tile(j):
        sop(lambda e, j=j: e.activation(
            atile[:, AEXP + 2 * j:AEXP + 2 * j + 2],
            atile[:, 2 * j:2 * j + 2], AF.Exp),
            reads=["atile"], writes=["atile"])

    def s_aexp_total():
        sop(lambda e: e.activation(atile[:, 16:18], atile[:, 16:18], AF.Exp),
            reads=["atile"], writes=["atile"])

    def s_bnd_inv():
        sop(lambda e: e.activation(bstf[0:1, 3:4], bstf[0:1, 2:3], AF.Ln,
                                   bias=cns[0:1, C_EPS:C_EPS + 1]),
            reads=["bstf"], writes=["bstf"])
        sop(lambda e: e.activation(bstf[0:1, 4:5], bstf[0:1, 3:4], AF.Exp,
                                   scale=cns[0:1, C_NEGH:C_NEGH + 1]),
            reads=["bstf"], writes=["bstf"])
        sop(lambda e: e.activation(bstb[0:1, 1:2], bstf[0:1, 4:5], AF.Copy),
            reads=["bstf"], writes=["bstb"], inc="bnd3")

    def z_evs(tt):
        rec = None
        for g in range(GC):
            key = ("wc", tt, g)
            bk = groups[gidx[key]]["bank"]
            extra = [("mm", gdone(("wc", tt, min(g + 1, GC - 1))))] \
                if g % 2 == 0 else []
            rec = sop(lambda e, bk=bk, tt=tt, g=g: e.activation(
                xbt[tt % 2][:, g * TT:(g + 1) * TT], ps[bk][:, :],
                AF.Identity, bias=cns[:, C_BC + g:C_BC + g + 1]),
                writes=[("xb", tt % 2, g)], extra=extra)
            brel(key, rec)
        waitrec[("zev", tt)] = rec

    def sq1(tt):
        extra = []
        if tt >= 2:
            extra = [("mm", gdone(("st2", tt - 2)))]
        elif tt == 1:
            extra = [("mm", gdone(("st1", 0)))]
        rec = sop(lambda e, tt=tt: e.activation(
            sqb[:, :].rearrange("p (g w) -> p g w", g=GC), xv(tt),
            AF.Square), reads=XB(tt % 2), writes=["sqb"], inc="sq1",
            extra=extra)
        waitrec[("sq1", tt)] = rec

    def p2_evs(kind, tt, lo, hi, dst, func, donesem, wkeys):
        for g in range(lo, hi):
            key = (kind, tt, g)
            bk = groups[gidx[key]]["bank"]
            extra = [("mm", gdone((kind, tt, min(g + 1, hi - 1))))] \
                if g % 2 == 0 else []
            rec = sop(lambda e, bk=bk, g=g, dst=dst, func=func: e.activation(
                dst[:, g * TT:(g + 1) * TT], ps[bk][:, :], func),
                writes=[wkeys[g]],
                inc=(donesem if g == GC - 1 else None),
                extra=extra)
            brel(key, rec)

    def r_evs(tt, lo, hi):
        p2_evs("fr", tt, lo, hi, rb, AF.Sigmoid, "rev", RB)

    def k_evs(tt, lo, hi):
        p2_evs("fk", tt, lo, hi, kb, AF.Relu, "kev", KB)

    def kv_evs(tt):
        p2_evs("fv", tt, 0, GC, kb, AF.Copy, "kvev", KB)

    def mbf(which, tt):
        key = (f"st{which}", tt)
        bank = 4 if which == 1 else 5
        row = mbf1 if which == 1 else mbf2
        extra = [("mm", gdone(key))]
        if tt >= 1:
            extra.append(("mm", gdone((f"bc{which}m", tt - 1))))
        sop(lambda e, bank=bank, row=row: e.activation(
            row[0:1, :], ps[bank][0:1, :],
            AF.Identity, scale=cns[0:1, C_S1024:C_S1024 + 1]),
            writes=[f"mbf{which}"], inc=f"mbf{which}", extra=extra)

    def inv_s(which, tt):
        stf = stf1 if which == 1 else stf2
        row = ivf1 if which == 1 else ivf2
        sk = f"stf{which}"
        sop(lambda e, stf=stf: e.activation(
            stf[0:1, 0:TT], stf[0:1, TT:2 * TT], AF.Ln,
            bias=cns[0:1, C_EPS:C_EPS + 1]), reads=[sk], writes=[sk])
        sop(lambda e, stf=stf: e.activation(
            stf[0:1, TT:2 * TT], stf[0:1, 0:TT], AF.Exp,
            scale=cns[0:1, C_NEGH:C_NEGH + 1]), reads=[sk], writes=[sk])
        extra = ([("mm", gdone((f"bc{which}i", tt - 1)))] if tt >= 1 else [])
        sop(lambda e, stf=stf, row=row: e.activation(
            row[0:1, :], stf[0:1, TT:2 * TT], AF.Copy),
            reads=[sk], writes=[f"ivf{which}"], inc=f"invbf{which}",
            extra=extra)

    def mb1_ev(tt):
        key = ("bc1m", tt)
        bk = groups[gidx[key]]["bank"]
        rec = sop(lambda e, bk=bk: e.copy(mb1sb[:, :], ps[bk][:, :]),
                  writes=["mb1"], inc="mbev1",
                  extra=[("mm", gdone(key))])
        brel(key, rec)

    def ib1_ev(tt):
        key = ("bc1i", tt)
        bk = groups[gidx[key]]["bank"]
        rec = sop(lambda e, bk=bk: e.copy(ib1sb[:, :], ps[bk][:, :]),
                  writes=["ib1"], inc="ibev1",
                  extra=[("mm", gdone(key))])
        brel(key, rec)

    def sq2(tt):
        extra = [("mm", gdone(("fv", tt, GD - 1)))]
        rec = sop(lambda e, tt=tt: e.activation(
            sqb[:, :].rearrange("p (g w) -> p g w", g=GC), xlv(tt),
            AF.Square), reads=[("xl", tt % 2)], writes=["sqb"], inc="sq2",
            extra=extra)
        waitrec[("sq2", tt)] = rec

    def v_p1_tile(j):
        for gs in range(GS):
            asl = alpha[:, gs * tpc + j * TT: gs * tpc + (j + 1) * TT]
            usl = ua[:, gs * tpc + j * TT: gs * tpc + (j + 1) * TT]
            vop(lambda e, asl=asl, usl=usl, gs=gs: e.tensor_tensor_scan(
                hsf[:, gs * TT:(gs + 1) * TT], asl, usl, zrow,
                OP.mult, OP.add),
                reads=[("al", j, gs), ("uu", j, gs)], writes=["hsf"])
            vop(lambda e, j=j, gs=gs: e.tensor_copy(
                etile[:, 2 * j + gs:2 * j + gs + 1],
                hsf[:, (gs + 1) * TT - 1:(gs + 1) * TT]),
                reads=["hsf"], writes=["etile"])
        for gs in range(GS):
            vop(lambda e, j=j, gs=gs: e.reduce_sum(
                atile[:, 2 * j + gs:2 * j + gs + 1],
                hsb[0][:, gs * TT:(gs + 1) * TT], mybir.AxisListType.X),
                reads=[("hsb", 0)], writes=["atile"])

    def v_asum_efold():
        vop(lambda e: e.reduce_sum(
            atile[:, 16:18],
            atile[:, 0:2 * NT].rearrange("p (j g) -> p g j", g=2),
            mybir.AxisListType.X), reads=["atile"], writes=["atile"])

    def v_efold_step(j):
        if j == 0:
            vop(lambda e: e.tensor_copy(ewk[:, 0:2], etile[:, 0:2]),
                reads=["etile"], writes=["ewk"])
        else:
            vop(lambda e, j=j: e.tensor_mul(
                ewk[:, 2:4], atile[:, AEXP + 2 * j:AEXP + 2 * j + 2],
                ewk[:, 0:2]),
                reads=["atile", "ewk"], writes=["ewk"])
            vop(lambda e, j=j: e.tensor_add(ewk[:, 0:2], ewk[:, 2:4],
                                            etile[:, 2 * j:2 * j + 2]),
                reads=["etile", "ewk"], writes=["ewk"])

    def v_chain():
        vop(lambda e: e.tensor_copy(agin[:, 0:2], atile[:, 16:18]),
            reads=["atile"], writes=["agin"])
        vop(lambda e: e.tensor_copy(agin[:, 2:4], ewk[:, 0:2]),
            reads=["ewk"], writes=["agin"], inc="chain")
        vop(lambda e: e.tensor_copy(hina[:, 0:2], zrow.broadcast_to([128, 2])),
            writes=["hina"], extra=[("ccd2", 16)])
        vop(lambda e: e.tensor_copy(hina[:, 2:4], agr[:, 2:4]),
            reads=["hina"], writes=["hina"])
        for q in (2, 3):
            pcol = (q - 1) * 4
            vop(lambda e, q=q, pcol=pcol: e.tensor_mul(
                hwk[:, q * 2:q * 2 + 2], agr[:, pcol:pcol + 2],
                hina[:, (q - 1) * 2:(q - 1) * 2 + 2]),
                reads=["hina"], writes=["hwk"])
            vop(lambda e, q=q, pcol=pcol: e.tensor_add(
                hina[:, q * 2:q * 2 + 2], hwk[:, q * 2:q * 2 + 2],
                agr[:, pcol + 2:pcol + 4]),
                reads=["hwk"], writes=["hina"])
        vop(lambda e: e.tensor_mul(hwk[:, :], hina[:, :],
                                   cns[:, C_HM:C_HM + 8]),
            reads=["hina"], writes=["hwk"])
        vop(lambda e: e.tensor_add(hwk[:, 0:4], hwk[:, 0:4], hwk[:, 4:8]),
            reads=["hwk"], writes=["hwk"])
        vop(lambda e: e.tensor_add(hsel[:, :], hwk[:, 0:2], hwk[:, 2:4]),
            reads=["hwk"], writes=["hsel"])
        vop(lambda e: e.tensor_copy(hselbf[:, :], hsel[:, :]),
            reads=["hsel"], writes=["hselbf"], inc="chain")
        vop(lambda e: e.tensor_copy(hinit[:, 0:2], hsel[:, :]),
            reads=["hsel"], writes=["hinit"])

    def hinit_rest():
        for t in range(1, NT):
            vop(lambda e, t=t: e.tensor_mul(
                hwk[:, 0:2], atile[:, AEXP + 2 * (t - 1):AEXP + 2 * t],
                hinit[:, 2 * (t - 1):2 * t]),
                reads=["atile", "hinit"], writes=["hwk"])
            vop(lambda e, t=t: e.tensor_add(
                hinit[:, 2 * t:2 * t + 2], hwk[:, 0:2],
                etile[:, 2 * (t - 1):2 * t]),
                reads=["hwk", "etile"], writes=["hinit"])

    def v_boundary():
        r_ = vop(lambda e: e.tensor_add(bt0[:, :], ps[6][:, 0:8],
                                        cns[:, C_BC:C_BC + 8]),
                 writes=["bt0"], inc="bnd1",
                 extra=[("mm", gdone(("bnd", 0)))])
        brel(("bnd", 0), r_)
        vop(lambda e: e.tensor_add(bt0[:, :], bt0[:, :],
                                   cns[:, C_XP:C_XP + 8]),
            reads=["bt0"], writes=["bt0"])
        vop(lambda e: e.tensor_add(bt1[:, 0:4], bt0[:, 0:4], bt0[:, 4:8]),
            reads=["bt0"], writes=["bt1"])
        vop(lambda e: e.tensor_add(bt1[:, 0:2], bt1[:, 0:2], bt1[:, 2:4]),
            reads=["bt1"], writes=["bt1"])
        vop(lambda e: e.tensor_add(bt1[:, 0:1], bt1[:, 0:1], bt1[:, 1:2]),
            reads=["bt1"], writes=["bt1"])
        vop(lambda e: e.tensor_copy(btb[:, 0:1], bt1[:, 0:1]),
            reads=["bt1"], writes=["btb"])
        vop(lambda e: e.tensor_mul(bt2[:, :], bt0[:, :], bt0[:, :]),
            reads=["bt0"], writes=["bt2"])
        vop(lambda e: e.tensor_add(bt1[:, 0:4], bt2[:, 0:4], bt2[:, 4:8]),
            reads=["bt2"], writes=["bt1"])
        vop(lambda e: e.tensor_add(bt1[:, 0:2], bt1[:, 0:2], bt1[:, 2:4]),
            reads=["bt1"], writes=["bt1"])
        vop(lambda e: e.tensor_add(bt1[:, 0:1], bt1[:, 0:1], bt1[:, 1:2]),
            reads=["bt1"], writes=["bt1"])
        vop(lambda e: e.tensor_copy(btb[:, 1:2], bt1[:, 0:1]),
            reads=["bt1"], writes=["btb"], inc="bnd1")
        vop(lambda e: e.tensor_scalar(bstf[0:1, 0:1], ps[4][0:1, 0:1],
                                      1.0 / D, None, OP.mult),
            writes=["bstf"], extra=[("mm", gdone(("bstatb", 0)))])
        vop(lambda e: e.tensor_copy(bstb[0:1, 0:1], bstf[0:1, 0:1]),
            reads=["bstf"], writes=["bstb"])
        vop(lambda e: e.tensor_mul(bstf[0:1, 1:2], bstf[0:1, 0:1],
                                   bstf[0:1, 0:1]),
            reads=["bstf"], writes=["bstf"])
        r_ = vop(lambda e: e.scalar_tensor_tensor(
            bstf[0:1, 2:3], ps[5][0:1, 0:1], 1.0 / D, bstf[0:1, 1:2],
            OP.mult, OP.subtract),
            reads=["bstf"], writes=["bstf"], inc="bnd2")
        brel(("bstata", 0), r_)
        brel(("bstatb", 0), r_)

    def v_bnd_apply():
        vop(lambda e: e.tensor_sub(bt1[:, :], bt0[:, :],
                                   ps[6][:, 0:1].broadcast_to([128, 8])),
            reads=["bt0"], writes=["bt1"],
            extra=[("mm", gdone(("bbci", 0)))])
        r_ = vop(lambda e: e.tensor_mul(
            bt1[:, :], bt1[:, :], ps[7][:, 0:1].broadcast_to([128, 8])),
            reads=["bt1"], writes=["bt1"], inc="bnd2")
        brel(("bbcm", 0), r_)
        brel(("bbci", 0), r_)
        vop(lambda e: e.tensor_scalar(bt1[:, :], bt1[:, :],
                                      cns[:, C_BM:C_BM + 1], None, OP.mult),
            reads=["bt1"], writes=["bt1"])
        vop(lambda e: e.tensor_copy(bndv(0).squeeze(2), bt1[:, :]),
            reads=["bt1"], writes=[("xl", 0)])

    def scan2(tt):
        for gs in range(GS):
            asl = alpha[:, gs * tpc + tt * TT: gs * tpc + (tt + 1) * TT]
            usl = ua[:, gs * tpc + tt * TT: gs * tpc + (tt + 1) * TT]
            extra = []
            if gs == 0 and tt >= 2:
                extra = [("mm", gdone(("wc", tt - 2, GD - 1)))]
            vop(lambda e, asl=asl, usl=usl, gs=gs, tt=tt:
                e.tensor_tensor_scan(hsb[tt % 2][:, gs * TT:(gs + 1) * TT],
                                     asl, usl,
                                     hinit[:, 2 * tt + gs:2 * tt + gs + 1],
                                     OP.mult, OP.add),
                reads=[("al", tt, gs), ("uu", tt, gs), "hinit"],
                writes=[("hsb", tt % 2)], extra=extra,
                inc=("hsbf" if gs == GS - 1 else None))

    def fin(which, tt):
        key = (f"st{which}", tt)
        sk = f"stf{which}"
        stf = stf1 if which == 1 else stf2
        mrow = mbf1 if which == 1 else mbf2
        bank = 4 if which == 1 else 5
        vop(lambda e, stf=stf, mrow=mrow: e.tensor_mul(
            stf[0:1, 0:TT], mrow[0:1, :], mrow[0:1, :]),
            reads=[f"mbf{which}"], writes=[sk])
        r_ = vop(lambda e, stf=stf, bank=bank: e.scalar_tensor_tensor(
            stf[0:1, TT:2 * TT], ps[bank][64:65, :], 1.0 / D,
            stf[0:1, 0:TT], OP.mult, OP.subtract),
            reads=[sk], writes=[sk], inc=f"fin{which}",
            extra=[("mm", gdone(key))])
        brel(key, r_)

    def apply2(tt):
        keym = ("bc2m", tt)
        keyi = ("bc2i", tt)
        bkm = groups[gidx[keym]]["bank"]
        bki = groups[gidx[keyi]]["bank"]
        for h in range(2):
            gsl = slice(4 * h, 4 * h + 4)
            extram = [("mm", gdone(keym))] if h == 0 else []
            if tt >= 1 and h == 0:
                extram.append(("store", 32 * tt))
            rm = vop(lambda e, tt=tt, bkm=bkm, gsl=gsl: e.tensor_sub(
                gv(outst)[:, gsl], xlv(tt)[:, gsl],
                ps[bkm][:, :].unsqueeze(1).broadcast_to([128, 4, TT])),
                reads=[("xl", tt % 2)], writes=["outst"], extra=extram)
            ri = vop(lambda e, bki=bki, gsl=gsl: e.tensor_mul(
                gv(outst)[:, gsl], gv(outst)[:, gsl],
                ps[bki][:, :].unsqueeze(1).broadcast_to([128, 4, TT])),
                reads=["outst"], writes=["outst"], inc="ap2",
                extra=([("mm", gdone(keyi))] if h == 0 else []))
            if h == 1:
                brel(keym, rm)
                brel(keyi, ri)

    def kact(tt):
        extra = ([("mm", gdone(("fv", tt - 1, GD - 1)))] if tt >= 1 else [])
        if tt + 1 < NT:
            extra.append(("mm", gdone(("st1", tt + 1))))
        else:
            extra.append(("mm", gdone(("st2", tt - 1))))
        vop(lambda e: e.tensor_mul(sqb[:, :], kb[:, :], kb[:, :]),
            reads=KB, writes=["sqb"], inc="kact", extra=extra)

    def apply1(tt):
        for g in range(GC):
            extra = [("mm", gdone(("st2", tt - 2)))] \
                if (tt >= 2 and g == 0) else []
            vop(lambda e, tt=tt, g=g: e.tensor_sub(
                xlv(tt)[:, g], xv(tt)[:, g], mb1sb[:, :]),
                reads=[("xb", tt % 2, g), "mb1"], writes=[("xl", tt % 2)],
                extra=extra)
            vop(lambda e, tt=tt, g=g: e.tensor_mul(
                xlv(tt)[:, g], xlv(tt)[:, g], ib1sb[:, :]),
                reads=["ib1"], writes=[("xl", tt % 2)],
                inc=("ap1" if g == GC - 1 else None))

    def v_diff(tt):
        extra = ([("mm", gdone(("fk", tt - 1, GD - 1)))] if tt >= 1 else [])
        vop(lambda e, tt=tt: e.tensor_sub(gv(xkb), xlv(tt), xpv(tt)),
            reads=[("xl", tt % 2)], writes=XRK("xk"), inc="diff",
            extra=extra)

    def v_adds(tt):
        extra = ([("mm", gdone(("fr", tt - 1, GD - 1)))] if tt >= 1 else [])
        for g in range(GC):
            vop(lambda e, tt=tt, g=g: e.scalar_tensor_tensor(
                gv(xrb)[:, g], gv(xkb)[:, g],
                cns[:, C_MUR + g:C_MUR + g + 1], xpv(tt)[:, g],
                OP.mult, OP.add),
                reads=[("xk", g), ("xl", tt % 2)], writes=[("xr", g)],
                inc="xr", extra=(extra if g == 0 else []))
        for g in range(GC):
            vop(lambda e, tt=tt, g=g: e.scalar_tensor_tensor(
                gv(xkb)[:, g], gv(xkb)[:, g],
                cns[:, C_MUK + g:C_MUK + g + 1], xpv(tt)[:, g],
                OP.mult, OP.add),
                reads=[("xl", tt % 2)], writes=[("xk", g)], inc="xk")

    def bnd_save(tt):
        vop(lambda e, tt=tt: e.tensor_copy(bndv(tt + 1), lastv(tt)),
            reads=[("xl", tt % 2)], writes=[("xl", (tt + 1) % 2)])

    def fold2(tt):
        extra = [("mm", gdone(("st2", tt - 1)))] if tt >= 1 else []
        rz = vop(lambda e, tt=tt: e.tensor_add(
            foldb[:, 0:4 * TT].rearrange("p (g w) -> p g w", g=4),
            xlv(tt)[:, 0:4], xlv(tt)[:, 4:8]),
            reads=[("xl", tt % 2)], writes=["foldb"], extra=extra)
        waitrec[("fz2", tt)] = rz
        rq = vop(lambda e: e.tensor_add(
            foldb[:, 4 * TT:8 * TT].rearrange("p (g w) -> p g w", g=4),
            gv(sqb)[:, 0:4], gv(sqb)[:, 4:8]),
            reads=["sqb"], writes=["foldb"])
        waitrec[("fq2", tt)] = rq

    def rkv_z2(tt):
        vop(lambda e: e.tensor_mul(kb[:, :], rb[:, :], kb[:, :]),
            reads=RB + KB, writes=KB, inc="rkv")
        rec = vop(lambda e, tt=tt: e.tensor_add(xlv(tt), gv(kb), xlv(tt)),
                  reads=KB + [("xl", tt % 2)], writes=[("xl", tt % 2)],
                  inc="z2")
        waitrec[("z2", tt)] = rec

    def rkv_z2_sq2_tail(tt):
        for h in range(2):
            kh = [("kb", g) for g in range(4 * h, 4 * h + 4)]
            rh = [("rb", g) for g in range(4 * h, 4 * h + 4)]
            sl = slice(4 * h * TT, (4 * h + 4) * TT)
            gsl = slice(4 * h, 4 * h + 4)
            vop(lambda e, sl=sl: e.tensor_mul(kb[:, sl], rb[:, sl],
                                              kb[:, sl]),
                reads=rh + kh, writes=kh, inc="rkv")
            rec = vop(lambda e, tt=tt, gsl=gsl: e.tensor_add(
                xlv(tt)[:, gsl], gv(kb)[:, gsl], xlv(tt)[:, gsl]),
                reads=kh + [("xl", tt % 2)], writes=[("xl", tt % 2)],
                inc="z2")
            waitrec[("z2h", tt, h)] = rec
            srec = sop(lambda e, tt=tt, gsl=gsl, sl=sl: e.activation(
                sqb[:, sl].rearrange("p (g w) -> p g w", g=4), xlv(tt)[:, gsl],
                AF.Square), reads=[("xl", tt % 2)], writes=["sqb"], inc="sq2",
                extra=([("mm", gdone(("fv", tt, GD - 1)))] if h == 0 else []))
            waitrec[("sq2h", tt, h)] = srec
        waitrec[("z2", tt)] = rec
        waitrec[("sq2", tt)] = srec

    for j in range(NT):
        s_p1_tile(j)
        v_p1_tile(j)
        s_aexp_tile(j)
        v_efold_step(j)
    v_asum_efold()
    s_aexp_total()
    v_chain()
    scan2(0)
    z_evs(0)
    hinit_rest()
    scan2(1)
    sq1(0)
    z_evs(1)
    v_boundary()
    s_bnd_inv()
    v_bnd_apply()
    mbf(1, 0)
    fin(1, 0)
    inv_s(1, 0)
    mb1_ev(0)
    ib1_ev(0)
    apply1(0)
    v_diff(0)
    v_adds(0)
    bnd_save(0)
    for c in range(NT):
        if c + 2 < NT:
            scan2(c + 2)
        r_evs(c, 0, 4)
        if c >= 1:
            sq2(c - 1)
            fold2(c - 1)
        r_evs(c, 4, 8)
        if c >= 1:
            mbf(2, c - 1)
            fin(2, c - 1)
        if c + 1 < NT:
            sq1(c + 1)
        if c >= 1:
            inv_s(2, c - 1)
            apply2(c - 1)
        k_evs(c, 0, 4)
        if c + 1 < NT:
            mbf(1, c + 1)
            fin(1, c + 1)
            inv_s(1, c + 1)
        k_evs(c, 4, 8)
        kact(c)
        if c + 1 < NT:
            mb1_ev(c + 1)
            ib1_ev(c + 1)
        if c + 2 < NT:
            z_evs(c + 2)
        if c + 1 < NT:
            apply1(c + 1)
            v_diff(c + 1)
            v_adds(c + 1)
            bnd_save(c + 1)
        kv_evs(c)
        if c < NT - 1:
            rkv_z2(c)
        else:
            rkv_z2_sq2_tail(c)
    mbf(2, NT - 1)
    fin(2, NT - 1)
    inv_s(2, NT - 1)
    apply2(NT - 1)


    with nc.Block() as block:

        @block.tensor
        def _(t):
            for g in groups:
                for s, v in g["waits"]:
                    if s == "REC":
                        nm, val = waitrec[v]
                        t.wait_ge(sems[nm], val)
                    else:
                        t.wait_ge(sems[s], v)
                if g["use"] is not None and g["use"] >= 1:
                    for nm, val in bankrec[(g["bank"], g["use"] - 1)]:
                        t.wait_ge(sems[nm], val)
                i = g["emit"](t)
                i.then_inc(sems["mm"], 1)

        def replay(eng, lane):
            for item in plan[lane]:
                _, fn, waits, nm, incby = item
                for wn, wv in waits:
                    eng.wait_ge(sems[wn], wv)
                fn(eng).then_inc(sems[nm], incby)

        @block.scalar
        def _(s):
            replay(s, "scalar")

        @block.vector
        def _(v):
            replay(v, "vector")

        @block.gpsimd
        def _(g):
            def dma(out, in_, semname):
                g.dma_start(out=out, in_=in_).then_inc(sems[semname], 16)

            def xload(k):
                tt = k % NT
                if k >= 2:
                    if k < NT + 2:
                        g.wait_ge(sems["mm"], gdone(("p1u", (k - 2) % NT, GS - 1)))
                    else:
                        g.wait_ge(sems["ap1"], (k - NT - 2) + 1)
                dma(xbt[k % 2][:, :],
                    xT[:, tt * GC * TT:(tt + 1) * GC * TT], f"xl{k % 2}")

            g.dma_start(out=warm_in, in_=consts[0:128, 0:4]
                        ).then_inc(sems["ccz"], 16)
            g.wait_ge(sems["ccz"], 16)
            g.collective_compute(
                "AllGather", OP.bypass,
                replica_groups=[[0, 1, 2, 3], [4, 5, 6, 7]],
                ins=[warm_in.opt()], outs=[cc_out.opt()],
            ).then_inc(sems["ccw"], 1)
            g.collective_compute(
                "AllGather", OP.bypass,
                replica_groups=[[0, 1, 2, 3], [4, 5, 6, 7]],
                ins=[warm_in.opt()], outs=[warm_out2.opt()],
            ).then_inc(sems["ccw"], 1)
            dma(wg_sb[:, :], wgT, "wg_l")
            xload(0)
            dma(wb_sb[:, :], wbT, "wb_l")
            dma(cns[:, :], consts, "cns_l")
            xload(1)
            if NT > 2:
                xload(2)
            dma(idn[:, :], idn_h, "idn_l")
            dma(onesb[:, :], onesb_h, "ones_l")
            for j in range(3, NT):
                xload(j)
            dma(wc_sb[:, :], wcT, "wc_l")
            xload(NT)
            xload(NT + 1)

            g.wait_ge(sems["ccw"], 2)
            g.wait_ge(sems["ccd1"], 16)
            g.collective_compute(
                "AllGather", OP.bypass,
                replica_groups=[[0, 1, 2, 3], [4, 5, 6, 7]],
                ins=[cc_in.opt()], outs=[cc_out.opt()],
            ).then_inc(sems["ccc"], 1)
            g.wait_ge(sems["ccc"], 1)

            dma(wr_sb[:, :], wrT, "wr_l")
            dma(wk_sb[:, :], wkT, "wk_l")
            dma(wv_sb[:, :], wvT, "wv_l")

            for k in range(NT + 2, 2 * NT):
                xload(k)

        @block.sync
        def _(sy):
            sy.wait_ge(sems["ccw"], 2)
            sy.wait_ge(sems["chain"], 1)
            sy.dma_start(out=cc_in, in_=agin[:, :]).then_inc(sems["ccd1"], 16)
            sy.wait_ge(sems["ccc"], 1)
            sy.dma_start(out=agr[:, :].rearrange("p (r c) -> p r c", r=CHUNKS),
                         in_=cc_out.rearrange("(r p) c -> p r c", p=128)
                         ).then_inc(sems["ccd2"], 16)
            for tt in range(NT):
                for h in range(2):
                    sy.wait_ge(sems["ap2"], 2 * tt + h + 1)
                    sy.dma_start(
                        out=yT[:, (tt * GC + 4 * h) * TT:
                                (tt * GC + 4 * h + 4) * TT],
                        in_=outst[:, 4 * h * TT:(4 * h + 4) * TT]
                    ).then_inc(sems["store"], 16)
            sy.wait_ge(sems["store"], 32 * NT)

    return nc


def _prep_inputs(inputs, NT):
    x = np.ascontiguousarray(np.asarray(inputs["L0_attn_x"], dtype=np.float32))
    Wg = np.asarray(inputs["L0_attn_Wg"], dtype=np.float32)
    bg = np.asarray(inputs["L0_attn_bg"], dtype=np.float32)
    WB = np.asarray(inputs["L0_attn_WB"], dtype=np.float32)
    bB = np.asarray(inputs["L0_attn_bB"], dtype=np.float32)
    WC = np.asarray(inputs["L0_attn_WC"], dtype=np.float32)
    bC = np.asarray(inputs["L0_attn_bC"], dtype=np.float32)
    Wr = np.asarray(inputs["L0_ffn_W_r"], dtype=np.float32)
    Wk = np.asarray(inputs["L0_ffn_W_k"], dtype=np.float32)
    Wv = np.asarray(inputs["L0_ffn_W_v"], dtype=np.float32)
    mur = np.asarray(inputs["L0_ffn_mu_r"], dtype=np.float32)
    muk = np.asarray(inputs["L0_ffn_mu_k"], dtype=np.float32)

    tpc = NT * TT

    def lhst(wT, ng):
        return np.ascontiguousarray(
            wT.reshape(ng, 128, wT.shape[1]).transpose(1, 0, 2)
            .reshape(128, ng * wT.shape[1]))

    wgT = lhst(np.ascontiguousarray(Wg.T), GC)
    wbT = lhst(np.ascontiguousarray(WB.T), GC)
    wcT = lhst(np.ascontiguousarray(WC.T), GS)
    wrT = lhst(np.ascontiguousarray(Wr.T), GC)
    wkT = lhst(np.ascontiguousarray(Wk.T), GC)
    wvT = lhst(np.ascontiguousarray(Wv.T), GC)
    idn = np.eye(128, dtype=np.float32)

    in_maps = []
    for c in range(NCORE):
        b, q = c // CHUNKS, c % CHUNKS
        t0 = q * tpc
        xsh = np.ascontiguousarray(
            x[b, t0:t0 + tpc, :].reshape(NT, TT, GC, 128)
            .transpose(3, 0, 2, 1).reshape(128, NT * GC * TT))
        cns = np.zeros((128, NCONST), np.float32)
        cns[:, C_BG:C_BG + 2] = bg.reshape(2, 128).T
        cns[:, C_BB:C_BB + 2] = bB.reshape(2, 128).T
        cns[:, C_BC:C_BC + 8] = bC.reshape(8, 128).T
        cns[:, C_MUR:C_MUR + 8] = mur.reshape(8, 128).T
        cns[:, C_MUK:C_MUK + 8] = muk.reshape(8, 128).T
        hm = np.zeros((128, 16), np.float32)
        hm[:, 2 * (c % CHUNKS):2 * (c % CHUNKS) + 2] = 1.0
        cns[:, C_HM:C_HM + 16] = hm
        if q > 0:
            cns[:, C_XP:C_XP + 8] = x[b, t0 - 1, :].reshape(8, 128).T
            cns[:, C_BM] = 1.0
        cns[:, C_EPS] = 1e-5
        cns[:, C_S1024] = 1.0 / D
        cns[:, C_NEGH] = -0.5
        cns[:, C_ONE] = 1.0
        cns[:, C_NBG:C_NBG + 2] = -bg.reshape(2, 128).T
        cns[:, C_NEG1] = -1.0
        import ml_dtypes
        in_maps.append(dict(
            xT=xsh, wgT=wgT, wbT=wbT, wcT=wcT, wrT=wrT, wkT=wkT, wvT=wvT,
            consts=cns, idn_h=idn,
            onesb_h=np.ones((128, 130), dtype=ml_dtypes.bfloat16)))
    return in_maps


def _out_chunk(y2, NT):
    return (np.asarray(y2).reshape(128, NT, GC, TT)
            .transpose(1, 3, 2, 0).reshape(NT * TT, D))


def kernel(**inputs):
    NT = TPC // TT
    nc = build_nc(NT)
    in_maps = _prep_inputs(inputs, NT)
    res = run_bass_kernel_spmd(nc, in_maps, core_ids=list(range(NCORE)))
    out = np.empty((B, L, D), np.float32)
    tpc = NT * TT
    for c in range(NCORE):
        b, q = c // CHUNKS, c % CHUNKS
        out[b, q * tpc:(q + 1) * tpc, :] = _out_chunk(res.results[c]["yT"], NT)
    return out


# version 25
# speedup vs baseline: 1.0197x; 1.0015x over previous
import numpy as np
import concourse.bass as bass
import concourse.mybir as mybir
from concourse.bass_utils import run_bass_kernel_spmd

F32 = mybir.dt.float32
BF16 = mybir.dt.bfloat16
AF = mybir.ActivationFunctionType
OP = mybir.AluOpType

B, L, D, S = 2, 16384, 1024, 256
NCORE = 8
CHUNKS = 4
TPC = L // CHUNKS
TT = 512
GC, GS, GD = 8, 2, 8

(C_BG, C_BB, C_BC, C_MUR, C_MUK, C_HM, C_XP, C_BM, C_EPS, C_S1024, C_NEGH,
 C_ZERO, C_ONE, C_NBG, C_NEG1) = (0, 2, 4, 12, 20, 28, 44, 52, 53, 54, 55, 56,
                                  57, 58, 60)
NCONST = 61


def build_nc(NT):
    tpc = NT * TT
    nc = bass.Bass("TRN2", target_bir_lowering=False, debug=False)

    xT = nc.dram_tensor("xT", [128, NT * GC * TT], F32,
                        kind="ExternalInput").ap()
    wgT = nc.dram_tensor("wgT", [128, GC * S], F32, kind="ExternalInput").ap()
    wbT = nc.dram_tensor("wbT", [128, GC * S], F32, kind="ExternalInput").ap()
    wcT = nc.dram_tensor("wcT", [128, GS * D], F32, kind="ExternalInput").ap()
    wrT = nc.dram_tensor("wrT", [128, GC * D], F32, kind="ExternalInput").ap()
    wkT = nc.dram_tensor("wkT", [128, GC * D], F32, kind="ExternalInput").ap()
    wvT = nc.dram_tensor("wvT", [128, GC * D], F32, kind="ExternalInput").ap()
    consts = nc.dram_tensor("consts", [128, NCONST], F32, kind="ExternalInput").ap()
    idn_h = nc.dram_tensor("idn_h", [128, 128], F32, kind="ExternalInput").ap()
    onesb_h = nc.dram_tensor("onesb_h", [128, 130], BF16, kind="ExternalInput").ap()
    yT = nc.dram_tensor("yT", [128, NT * GC * TT], F32,
                        kind="ExternalOutput").ap()
    cc_in = nc.dram_tensor("cc_in", [128, 4], F32, kind="Internal").ap()
    warm_in = nc.dram_tensor("warm_in", [128, 4], F32, kind="Internal").ap()
    warm_out2 = nc.dram_tensor("warm_out2", [128 * NCORE, 4], F32,
                               kind="Internal").ap()
    cc_out = nc.dram_tensor("cc_out", [128 * NCORE, 4], F32, kind="Internal").ap()

    A = nc.alloc_sbuf_tensor
    alpha = A("alpha", [128, GS * tpc], BF16)
    ua = A("ua", [128, GS * tpc], BF16)
    u_wgwb_xr = A("u_wgwb_xr", [128, 2 * GC * S], BF16)
    wg_sb = u_wgwb_xr[:, 0:GC * S]
    wb_sb = u_wgwb_xr[:, GC * S:2 * GC * S]
    wc_sb = A("wc_sb", [128, GS * D], BF16)
    wr_sb = A("wr_sb", [128, GC * D], BF16)
    wk_sb = A("wk_sb", [128, GC * D], BF16)
    wv_sb = A("wv_sb", [128, GC * D], BF16)
    xbt = [A("xbt0", [128, GC * TT], BF16), A("xbt1", [128, GC * TT], BF16)]
    xlnb = [A("xlnb0", [128, GC * (TT + 1)], BF16),
            A("xlnb1", [128, GC * (TT + 1)], BF16)]
    xrb = u_wgwb_xr[:, :]
    xkb = A("xkb", [128, GC * TT], BF16)
    rb = A("rb", [128, GC * TT], BF16)
    kb = A("kb", [128, GC * TT], BF16)
    sqb = A("sqb", [128, GC * TT], BF16)
    hsf = A("hsf", [128, GS * TT], F32)
    hsb = [A("hsb0", [128, GS * TT], BF16), A("hsb1", [128, GS * TT], BF16)]
    mb1sb = A("mb1sb", [128, TT], BF16)
    ib1sb = A("ib1sb", [128, TT], BF16)
    foldb = A("foldb", [128, 8 * TT], BF16)
    outst = A("outst", [128, GC * TT], F32)
    cns = A("cns", [128, NCONST], F32)
    idn = A("idn", [128, 128], BF16)
    onesb = A("onesb", [128, 130], BF16)
    agin = A("agin", [128, 4], F32)
    agr = A("agr", [128, 4 * NCORE], F32)
    hina = A("hina", [128, 16], F32)
    hwk = A("hwk", [128, 16], F32)
    hsel = A("hsel", [128, 2], F32)
    hselbf = A("hselbf", [128, 2], BF16)
    hinit = A("hinit", [128, 16], F32)
    atile = A("atile", [128, 36], F32)
    etile = A("etile", [128, 16], F32)
    ewk = A("ewk", [128, 4], F32)
    bt0 = A("bt0", [128, 8], F32)
    bt1 = A("bt1", [128, 8], F32)
    bt2 = A("bt2", [128, 8], F32)
    btb = A("btb", [128, 2], BF16)
    stf1 = A("stf1", [1, 2 * TT], F32)
    stf2 = A("stf2", [1, 2 * TT], F32)
    mbf1 = A("mbf1", [1, TT], BF16)
    ivf1 = A("ivf1", [1, TT], BF16)
    mbf2 = A("mbf2", [1, TT], BF16)
    ivf2 = A("ivf2", [1, TT], BF16)
    bstf = A("bstf", [1, 8], F32)
    bstb = A("bstb", [1, 2], BF16)
    mubf = A("mubf", [128, 16], BF16)

    P = nc.alloc_psum_tensor
    ps = [P(f"ps{i}", [128, 512], F32) for i in range(8)]

    sems = {}
    def sem(name):
        sems[name] = nc.alloc_semaphore(name)
        return sems[name]
    for n in ("cns_l idn_l wg_l wb_l wc_l wr_l wk_l wv_l xl0 xl1 "
              "mm evp1 scan2 hsbf chain bnd1 bnd2 bnd3 "
              "sq1 sq2 mbf1 mbf2 fin1 fin2 "
              "invbf1 invbf2 mbev1 ibev1 diff xr xk kact rev kev "
              "kvev ap1 ap2 z2 rkv store ccd1 ccc ccd2 ccw ccz").split():
        sem(n)
    sbk = [sem(f"bk{i}") for i in range(8)]
    sem("ones_l")
    sem("sord")
    sem("vord")
    sem("gord")
    semcnt = {}
    def bump(nm):
        semcnt[nm] = semcnt.get(nm, 0) + 1
        return semcnt[nm]

    dep = {}
    plan = {"scalar": [], "vector": [], "gpsimd": []}

    def mkop(engname):
        def op(fn, reads=(), writes=(), inc=None, extra=(), incby=1):
            waits = {}
            for nm, val in extra:
                waits[nm] = max(waits.get(nm, 0), val)
            for b in reads:
                st = dep.get(b)
                if st and st[0]:
                    nm, val = st[0]
                    waits[nm] = max(waits.get(nm, 0), val)
            for b in writes:
                st = dep.get(b)
                if st:
                    if st[0]:
                        nm, val = st[0]
                        waits[nm] = max(waits.get(nm, 0), val)
                    for nm, val in st[1].items():
                        waits[nm] = max(waits.get(nm, 0), val)
            nm = inc or {"scalar": "sord", "vector": "vord",
                         "gpsimd": "gord"}[engname]
            semcnt[nm] = semcnt.get(nm, 0) + incby
            val = semcnt[nm]
            rec = (nm, val)
            plan[engname].append(("op", fn, sorted(waits.items()), nm, incby))
            for b in reads:
                st = dep.setdefault(b, [None, {}])
                st[1][nm] = max(st[1].get(nm, 0), val)
            for b in writes:
                dep[b] = [rec, {}]
            return rec
        return op

    bankrec = {}
    waitrec = {}

    def brel(key, rec):
        g = groups[gidx[key]]
        bankrec.setdefault((g["bank"], g["use"]), []).append(rec)

    groups = []
    gidx = {}
    buse = [0] * 8

    def add_group(kind, tile, bank, waits, emit, key=None, nobank=False):
        g = dict(kind=kind, tile=tile, bank=bank, waits=list(waits),
                 emit=emit, idx=len(groups),
                 use=(None if nobank else buse[bank]))
        if not nobank:
            buse[bank] += 1
        groups.append(g)
        gidx[key or (kind, tile)] = g["idx"]
        return g

    ones_col = onesb[:, 0:1]
    ones_row = onesb[0:1, 2:130]

    def xv(t):
        return xbt[t % 2][:, :].rearrange("p (g w) -> p g w", g=GC)

    def xlv(t):
        return xlnb[t % 2][:, :].rearrange("p (g w) -> p g w", g=GC)[:, :, 1:TT + 1]

    def xpv(t):
        return xlnb[t % 2][:, :].rearrange("p (g w) -> p g w", g=GC)[:, :, 0:TT]

    def bndv(t):
        return xlnb[t % 2][:, :].rearrange("p (g w) -> p g w", g=GC)[:, :, 0:1]

    def lastv(t):
        return xlnb[t % 2][:, :].rearrange("p (g w) -> p g w", g=GC)[:, :, TT:TT + 1]

    for j in range(NT):
        for kind, wsb in (("a", wg_sb), ("u", wb_sb)):
            for gs in range(GS):
                bank = 4 * (j % 2) + (0 if kind == "a" else 2) + gs
                def emit(t, j=j, gs=gs, wsb=wsb, bank=bank):
                    for gc in range(GC):
                        i = t.matmul(ps[bank][:, :],
                                     wsb[:, gc * S + gs * 128: gc * S + (gs + 1) * 128],
                                     xbt[j % 2][:, gc * TT:(gc + 1) * TT],
                                     start=(gc == 0), stop=(gc == GC - 1))
                    return i
                evw = 6 * (j - 2) + (gs + 1 if kind == "a" else 3 + gs)
                add_group(f"p1{kind}", j, bank,
                          [(f"xl{j % 2}", 16 * (j // 2 + 1)),
                           ("wg_l" if kind == "a" else "wb_l", 16)]
                          + ([("evp1", evw)] if j >= 2 else []),
                          emit, key=(f"p1{kind}", j, gs), nobank=True)

    def emit_bnd(t):
        for gd in range(GD):
            for gs in range(GS):
                i = t.matmul(ps[6][:, gd:gd + 1],
                             wc_sb[:, gs * D + gd * 128: gs * D + (gd + 1) * 128],
                             hselbf[:, gs:gs + 1],
                             start=(gs == 0), stop=(gs == GS - 1))
        return i
    add_group("bnd", 0, 6, [("chain", 2), ("wc_l", 16)], emit_bnd)

    def emit_bstata(t):
        return t.matmul(ps[4][0:1, 0:1], ones_col[:, 0:1], btb[:, 0:1],
                        start=True, stop=True)

    def emit_bstatb(t):
        return t.matmul(ps[5][0:1, 0:1], ones_col[:, 0:1], btb[:, 1:2],
                        start=True, stop=True)

    def emit_bbcm(t):
        return t.matmul(ps[6][:, 0:1], ones_row[0:1, :], bstb[0:1, 0:1],
                        start=True, stop=True)

    def emit_bbci(t):
        return t.matmul(ps[7][:, 0:1], ones_row[0:1, :], bstb[0:1, 1:2],
                        start=True, stop=True)

    def add_boundary_stats():
        add_group("bstata", 0, 4, [("bnd1", 2), ("ones_l", 16)], emit_bstata)
        add_group("bstatb", 0, 5, [("bnd1", 2), ("ones_l", 16)], emit_bstatb)
        add_group("bbcm", 0, 6, [("bnd3", 1), ("ones_l", 16)], emit_bbcm)
        add_group("bbci", 0, 7, [("bnd3", 1), ("ones_l", 16)], emit_bbci)

    rot = [0]

    def add_data_group(kind, tile, g, waits, emit_fn):
        bank = rot[0]
        rot[0] = (rot[0] + 1) % 4
        def emit(t, emit_fn=emit_fn, bank=bank):
            return emit_fn(t, bank)
        add_group(kind, tile, bank, waits, emit, key=(kind, tile, g))

    def wc_group(tile, g):
        def emit(t, bank, tile=tile, g=g):
            for gs in range(GS):
                t.matmul(ps[bank][:, :],
                         wc_sb[:, gs * D + g * 128: gs * D + (g + 1) * 128],
                         hsb[tile % 2][:, gs * TT:(gs + 1) * TT],
                         start=(gs == 0), stop=False)
            return t.matmul(ps[bank][:, :], idn[:, :],
                            xbt[tile % 2][:, g * TT:(g + 1) * TT],
                            start=False, stop=True)
        k = NT + tile
        add_data_group("wc", tile, g,
                       [("hsbf", tile + 1), (f"xl{k % 2}", 16 * (k // 2 + 1)),
                        ("wc_l", 16), ("idn_l", 16)], emit)

    def ffn_group(kind, wsb, wsem, ssem, tile, g):
        def emit(t, bank, g=g, wsb=wsb):
            src = {"fr": xrb, "fk": xkb, "fv": sqb}[kind]
            for gc in range(GC):
                i = t.matmul(ps[bank][:, :],
                             wsb[:, gc * D + g * 128: gc * D + (g + 1) * 128],
                             src[:, gc * TT:(gc + 1) * TT],
                             start=(gc == 0), stop=(gc == GC - 1))
            return i
        sv = 8 * (tile + 1) if kind in ("fr", "fk") else (tile + 1)
        add_data_group(kind, tile, g, [(ssem, sv), (wsem, 16)], emit)

    def stats_group(which, tile):
        bank = 4 if which == 1 else 5
        last = (which == 2 and tile == NT - 1)
        folded = (which == 2 and not last)
        def emit(t, which=which, tile=tile, bank=bank, last=last,
                 folded=folded):
            if which == 1 and not folded:
                xb = xbt[tile % 2]
                for k in range(GC):
                    t.matmul(ps[bank][0:1, :], ones_col[:, 0:1],
                             xb[:, k * TT:(k + 1) * TT],
                             start=(k == 0), stop=(k == GC - 1))
            elif folded:
                for k in range(8):
                    row = 0 if k < 4 else 64
                    i = t.matmul(ps[bank][row:row + 1, :], ones_col[:, 0:1],
                                 foldb[:, k * TT:(k + 1) * TT],
                                 start=(k % 4 == 0), stop=(k % 4 == 3))
                return i
            else:
                xl = xlv(tile)
                for k in range(GC):
                    if k == 4:
                        nm, val = waitrec[("z2h", tile, 1)]
                        t.wait_ge(sems[nm], val)
                    t.matmul(ps[bank][0:1, :], ones_col[:, 0:1], xl[:, k],
                             start=(k == 0), stop=(k == GC - 1))
            for k in range(GC):
                if last and k in (0, 4):
                    nm, val = waitrec[("sq2h", tile, k // 4)]
                    t.wait_ge(sems[nm], val)
                i = t.matmul(ps[bank][64:65, :], ones_col[:, 0:1],
                             sqb[:, k * TT:(k + 1) * TT],
                             start=(k == 0), stop=(k == GC - 1))
            return i
        if which == 1:
            waits = [("ones_l", 16), ("REC", ("zev", tile)),
                     ("REC", ("sq1", tile))]
        elif last:
            waits = [("ones_l", 16), ("REC", ("z2h", tile, 0))]
        else:
            waits = [("ones_l", 16), ("REC", (f"fz{which}", tile)),
                     ("REC", (f"fq{which}", tile))]
        add_group(f"st{which}", tile, bank, waits, emit)

    def bc_group(which, mi, tile):
        bank = {"m": 6, "i": 7}[mi]
        row = {(1, "m"): mbf1, (1, "i"): ivf1, (2, "m"): mbf2, (2, "i"): ivf2}[(which, mi)]
        wsem = {"m": f"mbf{which}", "i": f"invbf{which}"}[mi]
        def emit(t, row=row, bank=bank):
            return t.matmul(ps[bank][:, :], ones_row[0:1, :], row[0:1, :],
                            start=True, stop=True)
        add_group(f"bc{which}{mi}", tile, bank,
                  [(wsem, tile + 1), ("ones_l", 16)], emit)

    for g in range(GD):
        wc_group(0, g)
    for g in range(GD):
        wc_group(1, g)
    add_boundary_stats()
    stats_group(1, 0)
    bc_group(1, "m", 0)
    bc_group(1, "i", 0)

    for c in range(NT):
        for g in range(GD):
            ffn_group("fr", wr_sb, "wr_l", "xr", c, g)
        if c >= 1:
            stats_group(2, c - 1)
        for g in range(0, 2):
            ffn_group("fk", wk_sb, "wk_l", "xk", c, g)
        if c >= 1:
            bc_group(2, "m", c - 1)
        for g in range(2, 4):
            ffn_group("fk", wk_sb, "wk_l", "xk", c, g)
        if c >= 1:
            bc_group(2, "i", c - 1)
        if c + 1 < NT:
            stats_group(1, c + 1)
        for g in range(4, 6):
            ffn_group("fk", wk_sb, "wk_l", "xk", c, g)
        if c + 1 < NT:
            bc_group(1, "m", c + 1)
        for g in range(6, 8):
            ffn_group("fk", wk_sb, "wk_l", "xk", c, g)
        if c + 1 < NT:
            bc_group(1, "i", c + 1)
        if c + 2 < NT:
            for g in range(GD):
                wc_group(c + 2, g)
        for g in range(GD):
            ffn_group("fv", wv_sb, "wv_l", "kact", c, g)
    stats_group(2, NT - 1)
    bc_group(2, "m", NT - 1)
    bc_group(2, "i", NT - 1)

    NGROUPS = len(groups)

    def gdone(key):
        return gidx[key] + 1

    XB = lambda p: [("xb", p, g) for g in range(GC)]
    KB = [("kb", g) for g in range(GC)]
    KBLO = [("kb", g) for g in range(4)]
    RB = [("rb", g) for g in range(GC)]
    XRK = lambda wh: [(wh, g) for g in range(GC)]

    sop = mkop("scalar")
    vop = mkop("vector")

    gv = lambda buf: buf[:, :].rearrange("p (g w) -> p g w", g=GC)

    def bcast_g(tile_128xTT):
        return tile_128xTT[:, :].unsqueeze(1).broadcast_to([128, GC, TT])

    zrow = cns[:, C_ZERO:C_ZERO + 1]

    def s_p1_tile(j):
        for gs in range(GS):
            key = ("p1a", j, gs)
            bk = groups[gidx[key]]["bank"]
            extra = [("mm", gdone(key))]
            if j == 0 and gs == 0:
                extra.append(("cns_l", 16))
            sop(lambda e, bk=bk, j=j, gs=gs: e.activation(
                alpha[:, gs * tpc + j * TT: gs * tpc + (j + 1) * TT],
                ps[bk][:, :], AF.Sigmoid,
                bias=cns[:, C_BG + gs:C_BG + gs + 1]),
                writes=[("al", j, gs)], inc="evp1", extra=extra)
        for gs in range(GS):
            key = ("p1u", j, gs)
            bk = groups[gidx[key]]["bank"]
            sop(lambda e, bk=bk, j=j, gs=gs: e.activation(
                ua[:, gs * tpc + j * TT: gs * tpc + (j + 1) * TT],
                ps[bk][:, :], AF.Identity,
                bias=cns[:, C_BB + gs:C_BB + gs + 1]),
                writes=[("uu", j, gs)], inc="evp1",
                extra=[("mm", gdone(key))])
        for gs in range(GS):
            sop(lambda e, j=j, gs=gs: e.activation(
                hsb[0][:, gs * TT:(gs + 1) * TT],
                alpha[:, gs * tpc + j * TT: gs * tpc + (j + 1) * TT],
                AF.Ln),
                reads=[("al", j, gs)], writes=[("hsb", 0)], inc="evp1")

    AEXP = 20

    def s_aexp_tile(j):
        sop(lambda e, j=j: e.activation(
            atile[:, AEXP + 2 * j:AEXP + 2 * j + 2],
            atile[:, 2 * j:2 * j + 2], AF.Exp),
            reads=["atile"], writes=["atile"])

    def s_aexp_total():
        sop(lambda e: e.activation(atile[:, 16:18], atile[:, 16:18], AF.Exp),
            reads=["atile"], writes=["atile"])

    def s_bnd_inv():
        sop(lambda e: e.activation(bstf[0:1, 3:4], bstf[0:1, 2:3], AF.Ln,
                                   bias=cns[0:1, C_EPS:C_EPS + 1]),
            reads=["bstf"], writes=["bstf"])
        sop(lambda e: e.activation(bstf[0:1, 4:5], bstf[0:1, 3:4], AF.Exp,
                                   scale=cns[0:1, C_NEGH:C_NEGH + 1]),
            reads=["bstf"], writes=["bstf"])
        sop(lambda e: e.activation(bstb[0:1, 1:2], bstf[0:1, 4:5], AF.Copy),
            reads=["bstf"], writes=["bstb"], inc="bnd3")

    def z_evs(tt):
        rec = None
        for g in range(GC):
            key = ("wc", tt, g)
            bk = groups[gidx[key]]["bank"]
            extra = [("mm", gdone(("wc", tt, min(g + 1, GC - 1))))] \
                if g % 2 == 0 else []
            rec = sop(lambda e, bk=bk, tt=tt, g=g: e.activation(
                xbt[tt % 2][:, g * TT:(g + 1) * TT], ps[bk][:, :],
                AF.Identity, bias=cns[:, C_BC + g:C_BC + g + 1]),
                writes=[("xb", tt % 2, g)], extra=extra)
            brel(key, rec)
        waitrec[("zev", tt)] = rec

    def sq1(tt):
        extra = []
        if tt >= 2:
            extra = [("mm", gdone(("st2", tt - 2)))]
        elif tt == 1:
            extra = [("mm", gdone(("st1", 0)))]
        rec = sop(lambda e, tt=tt: e.activation(
            sqb[:, :].rearrange("p (g w) -> p g w", g=GC), xv(tt),
            AF.Square), reads=XB(tt % 2), writes=["sqb"], inc="sq1",
            extra=extra)
        waitrec[("sq1", tt)] = rec

    def p2_evs(kind, tt, lo, hi, dst, func, donesem, wkeys):
        for g in range(lo, hi):
            key = (kind, tt, g)
            bk = groups[gidx[key]]["bank"]
            extra = [("mm", gdone((kind, tt, min(g + 1, hi - 1))))] \
                if g % 2 == 0 else []
            rec = sop(lambda e, bk=bk, g=g, dst=dst, func=func: e.activation(
                dst[:, g * TT:(g + 1) * TT], ps[bk][:, :], func),
                writes=[wkeys[g]],
                inc=(donesem if g == GC - 1 else None),
                extra=extra)
            brel(key, rec)

    def r_evs(tt, lo, hi):
        p2_evs("fr", tt, lo, hi, rb, AF.Sigmoid, "rev", RB)

    def k_evs(tt, lo, hi):
        p2_evs("fk", tt, lo, hi, kb, AF.Relu, "kev", KB)

    def kv_evs(tt):
        p2_evs("fv", tt, 0, GC, kb, AF.Copy, "kvev", KB)

    def mbf(which, tt):
        key = (f"st{which}", tt)
        bank = 4 if which == 1 else 5
        row = mbf1 if which == 1 else mbf2
        extra = [("mm", gdone(key))]
        if tt >= 1:
            extra.append(("mm", gdone((f"bc{which}m", tt - 1))))
        sop(lambda e, bank=bank, row=row: e.activation(
            row[0:1, :], ps[bank][0:1, :],
            AF.Identity, scale=cns[0:1, C_S1024:C_S1024 + 1]),
            writes=[f"mbf{which}"], inc=f"mbf{which}", extra=extra)

    def inv_s(which, tt):
        stf = stf1 if which == 1 else stf2
        row = ivf1 if which == 1 else ivf2
        sk = f"stf{which}"
        sop(lambda e, stf=stf: e.activation(
            stf[0:1, 0:TT], stf[0:1, TT:2 * TT], AF.Ln,
            bias=cns[0:1, C_EPS:C_EPS + 1]), reads=[sk], writes=[sk])
        sop(lambda e, stf=stf: e.activation(
            stf[0:1, TT:2 * TT], stf[0:1, 0:TT], AF.Exp,
            scale=cns[0:1, C_NEGH:C_NEGH + 1]), reads=[sk], writes=[sk])
        extra = ([("mm", gdone((f"bc{which}i", tt - 1)))] if tt >= 1 else [])
        sop(lambda e, stf=stf, row=row: e.activation(
            row[0:1, :], stf[0:1, TT:2 * TT], AF.Copy),
            reads=[sk], writes=[f"ivf{which}"], inc=f"invbf{which}",
            extra=extra)

    def mb1_ev(tt):
        key = ("bc1m", tt)
        bk = groups[gidx[key]]["bank"]
        rec = sop(lambda e, bk=bk: e.copy(mb1sb[:, :], ps[bk][:, :]),
                  writes=["mb1"], inc="mbev1",
                  extra=[("mm", gdone(key))])
        brel(key, rec)

    def ib1_ev(tt):
        key = ("bc1i", tt)
        bk = groups[gidx[key]]["bank"]
        rec = sop(lambda e, bk=bk: e.copy(ib1sb[:, :], ps[bk][:, :]),
                  writes=["ib1"], inc="ibev1",
                  extra=[("mm", gdone(key))])
        brel(key, rec)

    def sq2(tt):
        extra = [("mm", gdone(("fv", tt, GD - 1)))]
        rec = sop(lambda e, tt=tt: e.activation(
            sqb[:, :].rearrange("p (g w) -> p g w", g=GC), xlv(tt),
            AF.Square), reads=[("xl", tt % 2)], writes=["sqb"], inc="sq2",
            extra=extra)
        waitrec[("sq2", tt)] = rec

    def v_p1_tile(j):
        for gs in range(GS):
            asl = alpha[:, gs * tpc + j * TT: gs * tpc + (j + 1) * TT]
            usl = ua[:, gs * tpc + j * TT: gs * tpc + (j + 1) * TT]
            vop(lambda e, asl=asl, usl=usl, gs=gs: e.tensor_tensor_scan(
                hsf[:, gs * TT:(gs + 1) * TT], asl, usl, zrow,
                OP.mult, OP.add),
                reads=[("al", j, gs), ("uu", j, gs)], writes=["hsf"])
            vop(lambda e, j=j, gs=gs: e.tensor_copy(
                etile[:, 2 * j + gs:2 * j + gs + 1],
                hsf[:, (gs + 1) * TT - 1:(gs + 1) * TT]),
                reads=["hsf"], writes=["etile"])
        for gs in range(GS):
            vop(lambda e, j=j, gs=gs: e.reduce_sum(
                atile[:, 2 * j + gs:2 * j + gs + 1],
                hsb[0][:, gs * TT:(gs + 1) * TT], mybir.AxisListType.X),
                reads=[("hsb", 0)], writes=["atile"])

    def v_asum_efold():
        vop(lambda e: e.reduce_sum(
            atile[:, 16:18],
            atile[:, 0:2 * NT].rearrange("p (j g) -> p g j", g=2),
            mybir.AxisListType.X), reads=["atile"], writes=["atile"])

    def v_efold_step(j):
        if j == 0:
            vop(lambda e: e.tensor_copy(ewk[:, 0:2], etile[:, 0:2]),
                reads=["etile"], writes=["ewk"])
        else:
            vop(lambda e, j=j: e.tensor_mul(
                ewk[:, 2:4], atile[:, AEXP + 2 * j:AEXP + 2 * j + 2],
                ewk[:, 0:2]),
                reads=["atile", "ewk"], writes=["ewk"])
            vop(lambda e, j=j: e.tensor_add(ewk[:, 0:2], ewk[:, 2:4],
                                            etile[:, 2 * j:2 * j + 2]),
                reads=["etile", "ewk"], writes=["ewk"])

    def v_chain():
        vop(lambda e: e.tensor_copy(agin[:, 0:2], atile[:, 16:18]),
            reads=["atile"], writes=["agin"])
        vop(lambda e: e.tensor_copy(agin[:, 2:4], ewk[:, 0:2]),
            reads=["ewk"], writes=["agin"], inc="chain")
        first = [("ccd2", 16)]
        for band in range(2):
            r0 = band * 4
            vop(lambda e, r0=r0: e.tensor_copy(
                hina[:, r0 * 2:r0 * 2 + 2], zrow.broadcast_to([128, 2])),
                writes=["hina"], extra=first)
            first = []
            for q in range(1, 4):
                r = r0 + q
                pcol = (r - 1) * 4
                if q == 1:
                    vop(lambda e, r=r, pcol=pcol: e.tensor_copy(
                        hina[:, r * 2:r * 2 + 2], agr[:, pcol + 2:pcol + 4]),
                        reads=["hina"], writes=["hina"])
                else:
                    vop(lambda e, r=r, pcol=pcol: e.tensor_mul(
                        hwk[:, r * 2:r * 2 + 2], agr[:, pcol:pcol + 2],
                        hina[:, (r - 1) * 2:(r - 1) * 2 + 2]),
                        reads=["hina"], writes=["hwk"])
                    vop(lambda e, r=r, pcol=pcol: e.tensor_add(
                        hina[:, r * 2:r * 2 + 2], hwk[:, r * 2:r * 2 + 2],
                        agr[:, pcol + 2:pcol + 4]),
                        reads=["hwk"], writes=["hina"])
        vop(lambda e: e.tensor_mul(hwk[:, :], hina[:, :],
                                   cns[:, C_HM:C_HM + 16]),
            reads=["hina"], writes=["hwk"])
        vop(lambda e: e.tensor_add(hwk[:, 0:8], hwk[:, 0:8], hwk[:, 8:16]),
            reads=["hwk"], writes=["hwk"])
        vop(lambda e: e.tensor_add(hwk[:, 0:4], hwk[:, 0:4], hwk[:, 4:8]),
            reads=["hwk"], writes=["hwk"])
        vop(lambda e: e.tensor_add(hsel[:, :], hwk[:, 0:2], hwk[:, 2:4]),
            reads=["hwk"], writes=["hsel"])
        vop(lambda e: e.tensor_copy(hselbf[:, :], hsel[:, :]),
            reads=["hsel"], writes=["hselbf"], inc="chain")
        vop(lambda e: e.tensor_copy(hinit[:, 0:2], hsel[:, :]),
            reads=["hsel"], writes=["hinit"])

    def hinit_rest():
        for t in range(1, NT):
            vop(lambda e, t=t: e.tensor_mul(
                hwk[:, 0:2], atile[:, AEXP + 2 * (t - 1):AEXP + 2 * t],
                hinit[:, 2 * (t - 1):2 * t]),
                reads=["atile", "hinit"], writes=["hwk"])
            vop(lambda e, t=t: e.tensor_add(
                hinit[:, 2 * t:2 * t + 2], hwk[:, 0:2],
                etile[:, 2 * (t - 1):2 * t]),
                reads=["hwk", "etile"], writes=["hinit"])

    def v_boundary():
        r_ = vop(lambda e: e.tensor_add(bt0[:, :], ps[6][:, 0:8],
                                        cns[:, C_BC:C_BC + 8]),
                 writes=["bt0"], inc="bnd1",
                 extra=[("mm", gdone(("bnd", 0)))])
        brel(("bnd", 0), r_)
        vop(lambda e: e.tensor_add(bt0[:, :], bt0[:, :],
                                   cns[:, C_XP:C_XP + 8]),
            reads=["bt0"], writes=["bt0"])
        vop(lambda e: e.tensor_add(bt1[:, 0:4], bt0[:, 0:4], bt0[:, 4:8]),
            reads=["bt0"], writes=["bt1"])
        vop(lambda e: e.tensor_add(bt1[:, 0:2], bt1[:, 0:2], bt1[:, 2:4]),
            reads=["bt1"], writes=["bt1"])
        vop(lambda e: e.tensor_add(bt1[:, 0:1], bt1[:, 0:1], bt1[:, 1:2]),
            reads=["bt1"], writes=["bt1"])
        vop(lambda e: e.tensor_copy(btb[:, 0:1], bt1[:, 0:1]),
            reads=["bt1"], writes=["btb"])
        vop(lambda e: e.tensor_mul(bt2[:, :], bt0[:, :], bt0[:, :]),
            reads=["bt0"], writes=["bt2"])
        vop(lambda e: e.tensor_add(bt1[:, 0:4], bt2[:, 0:4], bt2[:, 4:8]),
            reads=["bt2"], writes=["bt1"])
        vop(lambda e: e.tensor_add(bt1[:, 0:2], bt1[:, 0:2], bt1[:, 2:4]),
            reads=["bt1"], writes=["bt1"])
        vop(lambda e: e.tensor_add(bt1[:, 0:1], bt1[:, 0:1], bt1[:, 1:2]),
            reads=["bt1"], writes=["bt1"])
        vop(lambda e: e.tensor_copy(btb[:, 1:2], bt1[:, 0:1]),
            reads=["bt1"], writes=["btb"], inc="bnd1")
        vop(lambda e: e.tensor_scalar(bstf[0:1, 0:1], ps[4][0:1, 0:1],
                                      1.0 / D, None, OP.mult),
            writes=["bstf"], extra=[("mm", gdone(("bstatb", 0)))])
        vop(lambda e: e.tensor_copy(bstb[0:1, 0:1], bstf[0:1, 0:1]),
            reads=["bstf"], writes=["bstb"])
        vop(lambda e: e.tensor_mul(bstf[0:1, 1:2], bstf[0:1, 0:1],
                                   bstf[0:1, 0:1]),
            reads=["bstf"], writes=["bstf"])
        r_ = vop(lambda e: e.scalar_tensor_tensor(
            bstf[0:1, 2:3], ps[5][0:1, 0:1], 1.0 / D, bstf[0:1, 1:2],
            OP.mult, OP.subtract),
            reads=["bstf"], writes=["bstf"], inc="bnd2")
        brel(("bstata", 0), r_)
        brel(("bstatb", 0), r_)

    def v_bnd_apply():
        vop(lambda e: e.tensor_sub(bt1[:, :], bt0[:, :],
                                   ps[6][:, 0:1].broadcast_to([128, 8])),
            reads=["bt0"], writes=["bt1"],
            extra=[("mm", gdone(("bbci", 0)))])
        r_ = vop(lambda e: e.tensor_mul(
            bt1[:, :], bt1[:, :], ps[7][:, 0:1].broadcast_to([128, 8])),
            reads=["bt1"], writes=["bt1"], inc="bnd2")
        brel(("bbcm", 0), r_)
        brel(("bbci", 0), r_)
        vop(lambda e: e.tensor_scalar(bt1[:, :], bt1[:, :],
                                      cns[:, C_BM:C_BM + 1], None, OP.mult),
            reads=["bt1"], writes=["bt1"])
        vop(lambda e: e.tensor_copy(bndv(0).squeeze(2), bt1[:, :]),
            reads=["bt1"], writes=[("xl", 0)])

    def scan2(tt):
        for gs in range(GS):
            asl = alpha[:, gs * tpc + tt * TT: gs * tpc + (tt + 1) * TT]
            usl = ua[:, gs * tpc + tt * TT: gs * tpc + (tt + 1) * TT]
            extra = []
            if gs == 0 and tt >= 2:
                extra = [("mm", gdone(("wc", tt - 2, GD - 1)))]
            vop(lambda e, asl=asl, usl=usl, gs=gs, tt=tt:
                e.tensor_tensor_scan(hsb[tt % 2][:, gs * TT:(gs + 1) * TT],
                                     asl, usl,
                                     hinit[:, 2 * tt + gs:2 * tt + gs + 1],
                                     OP.mult, OP.add),
                reads=[("al", tt, gs), ("uu", tt, gs), "hinit"],
                writes=[("hsb", tt % 2)], extra=extra,
                inc=("hsbf" if gs == GS - 1 else None))

    def fin(which, tt):
        key = (f"st{which}", tt)
        sk = f"stf{which}"
        stf = stf1 if which == 1 else stf2
        mrow = mbf1 if which == 1 else mbf2
        bank = 4 if which == 1 else 5
        vop(lambda e, stf=stf, mrow=mrow: e.tensor_mul(
            stf[0:1, 0:TT], mrow[0:1, :], mrow[0:1, :]),
            reads=[f"mbf{which}"], writes=[sk])
        r_ = vop(lambda e, stf=stf, bank=bank: e.scalar_tensor_tensor(
            stf[0:1, TT:2 * TT], ps[bank][64:65, :], 1.0 / D,
            stf[0:1, 0:TT], OP.mult, OP.subtract),
            reads=[sk], writes=[sk], inc=f"fin{which}",
            extra=[("mm", gdone(key))])
        brel(key, r_)

    def apply2(tt):
        keym = ("bc2m", tt)
        keyi = ("bc2i", tt)
        bkm = groups[gidx[keym]]["bank"]
        bki = groups[gidx[keyi]]["bank"]
        for h in range(2):
            gsl = slice(4 * h, 4 * h + 4)
            extram = [("mm", gdone(keym))] if h == 0 else []
            if tt >= 1 and h == 0:
                extram.append(("store", 32 * tt))
            rm = vop(lambda e, tt=tt, bkm=bkm, gsl=gsl: e.tensor_sub(
                gv(outst)[:, gsl], xlv(tt)[:, gsl],
                ps[bkm][:, :].unsqueeze(1).broadcast_to([128, 4, TT])),
                reads=[("xl", tt % 2)], writes=["outst"], extra=extram)
            ri = vop(lambda e, bki=bki, gsl=gsl: e.tensor_mul(
                gv(outst)[:, gsl], gv(outst)[:, gsl],
                ps[bki][:, :].unsqueeze(1).broadcast_to([128, 4, TT])),
                reads=["outst"], writes=["outst"], inc="ap2",
                extra=([("mm", gdone(keyi))] if h == 0 else []))
            if h == 1:
                brel(keym, rm)
                brel(keyi, ri)

    def kact(tt):
        extra = ([("mm", gdone(("fv", tt - 1, GD - 1)))] if tt >= 1 else [])
        if tt + 1 < NT:
            extra.append(("mm", gdone(("st1", tt + 1))))
        else:
            extra.append(("mm", gdone(("st2", tt - 1))))
        vop(lambda e: e.tensor_mul(sqb[:, :], kb[:, :], kb[:, :]),
            reads=KB, writes=["sqb"], inc="kact", extra=extra)

    def apply1(tt):
        for g in range(GC):
            extra = [("mm", gdone(("st2", tt - 2)))] \
                if (tt >= 2 and g == 0) else []
            vop(lambda e, tt=tt, g=g: e.tensor_sub(
                xlv(tt)[:, g], xv(tt)[:, g], mb1sb[:, :]),
                reads=[("xb", tt % 2, g), "mb1"], writes=[("xl", tt % 2)],
                extra=extra)
            vop(lambda e, tt=tt, g=g: e.tensor_mul(
                xlv(tt)[:, g], xlv(tt)[:, g], ib1sb[:, :]),
                reads=["ib1"], writes=[("xl", tt % 2)],
                inc=("ap1" if g == GC - 1 else None))

    def v_diff(tt):
        extra = ([("mm", gdone(("fk", tt - 1, GD - 1)))] if tt >= 1 else [])
        vop(lambda e, tt=tt: e.tensor_sub(gv(xkb), xlv(tt), xpv(tt)),
            reads=[("xl", tt % 2)], writes=XRK("xk"), inc="diff",
            extra=extra)

    def v_adds(tt):
        extra = ([("mm", gdone(("fr", tt - 1, GD - 1)))] if tt >= 1 else [])
        for g in range(GC):
            vop(lambda e, tt=tt, g=g: e.scalar_tensor_tensor(
                gv(xrb)[:, g], gv(xkb)[:, g],
                cns[:, C_MUR + g:C_MUR + g + 1], xpv(tt)[:, g],
                OP.mult, OP.add),
                reads=[("xk", g), ("xl", tt % 2)], writes=[("xr", g)],
                inc="xr", extra=(extra if g == 0 else []))
        for g in range(GC):
            vop(lambda e, tt=tt, g=g: e.scalar_tensor_tensor(
                gv(xkb)[:, g], gv(xkb)[:, g],
                cns[:, C_MUK + g:C_MUK + g + 1], xpv(tt)[:, g],
                OP.mult, OP.add),
                reads=[("xl", tt % 2)], writes=[("xk", g)], inc="xk")

    def bnd_save(tt):
        vop(lambda e, tt=tt: e.tensor_copy(bndv(tt + 1), lastv(tt)),
            reads=[("xl", tt % 2)], writes=[("xl", (tt + 1) % 2)])

    def fold2(tt):
        extra = [("mm", gdone(("st2", tt - 1)))] if tt >= 1 else []
        rz = vop(lambda e, tt=tt: e.tensor_add(
            foldb[:, 0:4 * TT].rearrange("p (g w) -> p g w", g=4),
            xlv(tt)[:, 0:4], xlv(tt)[:, 4:8]),
            reads=[("xl", tt % 2)], writes=["foldb"], extra=extra)
        waitrec[("fz2", tt)] = rz
        rq = vop(lambda e: e.tensor_add(
            foldb[:, 4 * TT:8 * TT].rearrange("p (g w) -> p g w", g=4),
            gv(sqb)[:, 0:4], gv(sqb)[:, 4:8]),
            reads=["sqb"], writes=["foldb"])
        waitrec[("fq2", tt)] = rq

    def rkv_z2(tt):
        vop(lambda e: e.tensor_mul(kb[:, :], rb[:, :], kb[:, :]),
            reads=RB + KB, writes=KB, inc="rkv")
        rec = vop(lambda e, tt=tt: e.tensor_add(xlv(tt), gv(kb), xlv(tt)),
                  reads=KB + [("xl", tt % 2)], writes=[("xl", tt % 2)],
                  inc="z2")
        waitrec[("z2", tt)] = rec

    def rkv_z2_sq2_tail(tt):
        for h in range(2):
            kh = [("kb", g) for g in range(4 * h, 4 * h + 4)]
            rh = [("rb", g) for g in range(4 * h, 4 * h + 4)]
            sl = slice(4 * h * TT, (4 * h + 4) * TT)
            gsl = slice(4 * h, 4 * h + 4)
            vop(lambda e, sl=sl: e.tensor_mul(kb[:, sl], rb[:, sl],
                                              kb[:, sl]),
                reads=rh + kh, writes=kh, inc="rkv")
            rec = vop(lambda e, tt=tt, gsl=gsl: e.tensor_add(
                xlv(tt)[:, gsl], gv(kb)[:, gsl], xlv(tt)[:, gsl]),
                reads=kh + [("xl", tt % 2)], writes=[("xl", tt % 2)],
                inc="z2")
            waitrec[("z2h", tt, h)] = rec
            srec = sop(lambda e, tt=tt, gsl=gsl, sl=sl: e.activation(
                sqb[:, sl].rearrange("p (g w) -> p g w", g=4), xlv(tt)[:, gsl],
                AF.Square), reads=[("xl", tt % 2)], writes=["sqb"], inc="sq2",
                extra=([("mm", gdone(("fv", tt, GD - 1)))] if h == 0 else []))
            waitrec[("sq2h", tt, h)] = srec
        waitrec[("z2", tt)] = rec
        waitrec[("sq2", tt)] = srec

    for j in range(NT):
        s_p1_tile(j)
        v_p1_tile(j)
        s_aexp_tile(j)
        v_efold_step(j)
    v_asum_efold()
    s_aexp_total()
    v_chain()
    scan2(0)
    z_evs(0)
    hinit_rest()
    scan2(1)
    sq1(0)
    z_evs(1)
    v_boundary()
    s_bnd_inv()
    v_bnd_apply()
    mbf(1, 0)
    fin(1, 0)
    inv_s(1, 0)
    mb1_ev(0)
    ib1_ev(0)
    apply1(0)
    v_diff(0)
    v_adds(0)
    bnd_save(0)
    for c in range(NT):
        if c + 2 < NT:
            scan2(c + 2)
        r_evs(c, 0, 4)
        if c >= 1:
            sq2(c - 1)
            fold2(c - 1)
        r_evs(c, 4, 8)
        if c >= 1:
            mbf(2, c - 1)
            fin(2, c - 1)
        if c + 1 < NT:
            sq1(c + 1)
        if c >= 1:
            inv_s(2, c - 1)
            apply2(c - 1)
        k_evs(c, 0, 4)
        if c + 1 < NT:
            mbf(1, c + 1)
            fin(1, c + 1)
            inv_s(1, c + 1)
        k_evs(c, 4, 8)
        kact(c)
        if c + 1 < NT:
            mb1_ev(c + 1)
            ib1_ev(c + 1)
        if c + 2 < NT:
            z_evs(c + 2)
        if c + 1 < NT:
            apply1(c + 1)
            v_diff(c + 1)
            v_adds(c + 1)
            bnd_save(c + 1)
        kv_evs(c)
        if c < NT - 1:
            rkv_z2(c)
        else:
            rkv_z2_sq2_tail(c)
    mbf(2, NT - 1)
    fin(2, NT - 1)
    inv_s(2, NT - 1)
    apply2(NT - 1)


    with nc.Block() as block:

        @block.tensor
        def _(t):
            for g in groups:
                for s, v in g["waits"]:
                    if s == "REC":
                        nm, val = waitrec[v]
                        t.wait_ge(sems[nm], val)
                    else:
                        t.wait_ge(sems[s], v)
                if g["use"] is not None and g["use"] >= 1:
                    for nm, val in bankrec[(g["bank"], g["use"] - 1)]:
                        t.wait_ge(sems[nm], val)
                i = g["emit"](t)
                i.then_inc(sems["mm"], 1)

        def replay(eng, lane):
            for item in plan[lane]:
                _, fn, waits, nm, incby = item
                for wn, wv in waits:
                    eng.wait_ge(sems[wn], wv)
                fn(eng).then_inc(sems[nm], incby)

        @block.scalar
        def _(s):
            replay(s, "scalar")

        @block.vector
        def _(v):
            replay(v, "vector")

        @block.gpsimd
        def _(g):
            def dma(out, in_, semname):
                g.dma_start(out=out, in_=in_).then_inc(sems[semname], 16)

            def xload(k):
                tt = k % NT
                if k >= 2:
                    if k < NT + 2:
                        g.wait_ge(sems["mm"], gdone(("p1u", (k - 2) % NT, GS - 1)))
                    else:
                        g.wait_ge(sems["ap1"], (k - NT - 2) + 1)
                dma(xbt[k % 2][:, :],
                    xT[:, tt * GC * TT:(tt + 1) * GC * TT], f"xl{k % 2}")

            g.dma_start(out=warm_in, in_=consts[0:128, 0:4]
                        ).then_inc(sems["ccz"], 16)
            g.wait_ge(sems["ccz"], 16)
            g.collective_compute(
                "AllGather", OP.bypass,
                replica_groups=[list(range(NCORE))],
                ins=[warm_in.opt()], outs=[cc_out.opt()],
            ).then_inc(sems["ccw"], 1)
            g.collective_compute(
                "AllGather", OP.bypass,
                replica_groups=[list(range(NCORE))],
                ins=[warm_in.opt()], outs=[warm_out2.opt()],
            ).then_inc(sems["ccw"], 1)
            dma(wg_sb[:, :], wgT, "wg_l")
            xload(0)
            dma(wb_sb[:, :], wbT, "wb_l")
            dma(cns[:, :], consts, "cns_l")
            xload(1)
            if NT > 2:
                xload(2)
            dma(idn[:, :], idn_h, "idn_l")
            dma(onesb[:, :], onesb_h, "ones_l")
            for j in range(3, NT):
                xload(j)
            dma(wc_sb[:, :], wcT, "wc_l")
            xload(NT)
            xload(NT + 1)

            g.wait_ge(sems["ccw"], 2)
            g.wait_ge(sems["ccd1"], 16)
            g.collective_compute(
                "AllGather", OP.bypass,
                replica_groups=[list(range(NCORE))],
                ins=[cc_in.opt()], outs=[cc_out.opt()],
            ).then_inc(sems["ccc"], 1)
            g.wait_ge(sems["ccc"], 1)

            dma(wr_sb[:, :], wrT, "wr_l")
            dma(wk_sb[:, :], wkT, "wk_l")
            dma(wv_sb[:, :], wvT, "wv_l")

            for k in range(NT + 2, 2 * NT):
                xload(k)

        @block.sync
        def _(sy):
            sy.wait_ge(sems["ccw"], 2)
            sy.wait_ge(sems["chain"], 1)
            sy.dma_start(out=cc_in, in_=agin[:, :]).then_inc(sems["ccd1"], 16)
            sy.wait_ge(sems["ccc"], 1)
            sy.dma_start(out=agr[:, :].rearrange("p (r c) -> p r c", r=NCORE),
                         in_=cc_out.rearrange("(r p) c -> p r c", p=128)
                         ).then_inc(sems["ccd2"], 16)
            for tt in range(NT):
                for h in range(2):
                    sy.wait_ge(sems["ap2"], 2 * tt + h + 1)
                    sy.dma_start(
                        out=yT[:, (tt * GC + 4 * h) * TT:
                                (tt * GC + 4 * h + 4) * TT],
                        in_=outst[:, 4 * h * TT:(4 * h + 4) * TT]
                    ).then_inc(sems["store"], 16)
            sy.wait_ge(sems["store"], 32 * NT)

    return nc


def _prep_inputs(inputs, NT):
    x = np.ascontiguousarray(np.asarray(inputs["L0_attn_x"], dtype=np.float32))
    Wg = np.asarray(inputs["L0_attn_Wg"], dtype=np.float32)
    bg = np.asarray(inputs["L0_attn_bg"], dtype=np.float32)
    WB = np.asarray(inputs["L0_attn_WB"], dtype=np.float32)
    bB = np.asarray(inputs["L0_attn_bB"], dtype=np.float32)
    WC = np.asarray(inputs["L0_attn_WC"], dtype=np.float32)
    bC = np.asarray(inputs["L0_attn_bC"], dtype=np.float32)
    Wr = np.asarray(inputs["L0_ffn_W_r"], dtype=np.float32)
    Wk = np.asarray(inputs["L0_ffn_W_k"], dtype=np.float32)
    Wv = np.asarray(inputs["L0_ffn_W_v"], dtype=np.float32)
    mur = np.asarray(inputs["L0_ffn_mu_r"], dtype=np.float32)
    muk = np.asarray(inputs["L0_ffn_mu_k"], dtype=np.float32)

    tpc = NT * TT

    def lhst(wT, ng):
        return np.ascontiguousarray(
            wT.reshape(ng, 128, wT.shape[1]).transpose(1, 0, 2)
            .reshape(128, ng * wT.shape[1]))

    wgT = lhst(np.ascontiguousarray(Wg.T), GC)
    wbT = lhst(np.ascontiguousarray(WB.T), GC)
    wcT = lhst(np.ascontiguousarray(WC.T), GS)
    wrT = lhst(np.ascontiguousarray(Wr.T), GC)
    wkT = lhst(np.ascontiguousarray(Wk.T), GC)
    wvT = lhst(np.ascontiguousarray(Wv.T), GC)
    idn = np.eye(128, dtype=np.float32)

    in_maps = []
    for c in range(NCORE):
        b, q = c // CHUNKS, c % CHUNKS
        t0 = q * tpc
        xsh = np.ascontiguousarray(
            x[b, t0:t0 + tpc, :].reshape(NT, TT, GC, 128)
            .transpose(3, 0, 2, 1).reshape(128, NT * GC * TT))
        cns = np.zeros((128, NCONST), np.float32)
        cns[:, C_BG:C_BG + 2] = bg.reshape(2, 128).T
        cns[:, C_BB:C_BB + 2] = bB.reshape(2, 128).T
        cns[:, C_BC:C_BC + 8] = bC.reshape(8, 128).T
        cns[:, C_MUR:C_MUR + 8] = mur.reshape(8, 128).T
        cns[:, C_MUK:C_MUK + 8] = muk.reshape(8, 128).T
        hm = np.zeros((128, 16), np.float32)
        hm[:, 2 * c:2 * c + 2] = 1.0
        cns[:, C_HM:C_HM + 16] = hm
        if q > 0:
            cns[:, C_XP:C_XP + 8] = x[b, t0 - 1, :].reshape(8, 128).T
            cns[:, C_BM] = 1.0
        cns[:, C_EPS] = 1e-5
        cns[:, C_S1024] = 1.0 / D
        cns[:, C_NEGH] = -0.5
        cns[:, C_ONE] = 1.0
        cns[:, C_NBG:C_NBG + 2] = -bg.reshape(2, 128).T
        cns[:, C_NEG1] = -1.0
        import ml_dtypes
        in_maps.append(dict(
            xT=xsh, wgT=wgT, wbT=wbT, wcT=wcT, wrT=wrT, wkT=wkT, wvT=wvT,
            consts=cns, idn_h=idn,
            onesb_h=np.ones((128, 130), dtype=ml_dtypes.bfloat16)))
    return in_maps


def _out_chunk(y2, NT):
    return (np.asarray(y2).reshape(128, NT, GC, TT)
            .transpose(1, 3, 2, 0).reshape(NT * TT, D))


def kernel(**inputs):
    NT = TPC // TT
    nc = build_nc(NT)
    in_maps = _prep_inputs(inputs, NT)
    res = run_bass_kernel_spmd(nc, in_maps, core_ids=list(range(NCORE)))
    out = np.empty((B, L, D), np.float32)
    tpc = NT * TT
    for c in range(NCORE):
        b, q = c // CHUNKS, c % CHUNKS
        out[b, q * tpc:(q + 1) * tpc, :] = _out_chunk(res.results[c]["yT"], NT)
    return out
